# version 14
# baseline (speedup 1.0000x reference)
import numpy as np
import concourse.bass as bass
import concourse.mybir as mybir
from concourse.bass_utils import run_bass_kernel_spmd
from contextlib import ExitStack

F32 = mybir.dt.float32
BF16 = mybir.dt.bfloat16
AF = mybir.ActivationFunctionType
ALU = mybir.AluOpType
AX = mybir.AxisListType
ds = bass.ds

T = 8192
D = 2048
NC = 16
NS = 32
RK = 64
NE = 16
PC = 256
NPC = T // PC
TB = 1024
NB = T // TB
TOK = 1024
NCORE = 8

ENGS = ("T", "V", "S", "G", "Q")


class Ctr:
    def __init__(self, sem):
        self.sem = sem
        self.v = 0

    def bump(self, n):
        self.v += n
        return self.v


class KB:
    def __init__(self):
        self.prog = {e: [] for e in ENGS}
        self.lastwait = {e: {} for e in ENGS}

    def emit(self, eng, mk, waits=(), inc=None, inc_n=1):
        lw = self.lastwait[eng]
        todo = []
        for cv in waits:
            if cv is None:
                continue
            c, v = cv
            if c is None or v is None or v <= 0:
                continue
            if lw.get(id(c), -1) >= v:
                continue
            lw[id(c)] = v
            todo.append((c, v))
        val = inc.bump(inc_n) if inc is not None else None

        def fn(e, todo=tuple(todo), mk=mk, inc=inc, inc_n=inc_n):
            for c, v in todo:
                e.wait_ge(c.sem, v)
            bi = mk(e)
            if inc is not None and bi is not None:
                bi.then_inc(inc.sem, inc_n)

        self.prog[eng].append(fn)
        return val

    def replay(self, block):
        secs = {"T": block.tensor, "V": block.vector, "S": block.scalar,
                "G": block.gpsimd, "Q": block.sync}
        for e in ENGS:
            fns = self.prog[e]
            if fns:
                secs[e](lambda eng, fns=tuple(fns): [fn(eng) for fn in fns])
            self.prog[e] = []
            self.lastwait[e] = {}


def build_nc(upto="all", dbg=()):
    nc = bass.Bass("TRN2", target_bir_lowering=False)

    def din(name, shape, dt=F32):
        return nc.dram_tensor(name, shape, dt, kind="ExternalInput").ap()

    x_in = din("x_in", [NPC, 128, NC, PC])
    x_own_in = din("x_own_in", [4, 128, NC, PC])
    x_sl_in = din("x_sl_in", [NPC, 128, 2, PC])
    wbc_in = din("wbc_in", [128, NC, 128])
    wdt_in = din("wdt_in", [RK, 2, 128])
    bdt_in = din("bdt_in", [128, 2])
    alog_in = din("alog_in", [128, 2, NS])
    dpar_in = din("dpar_in", [128, 2])
    gw_in = din("gw_in", [128, NC, NE])
    ebias_in = din("ebias_in", [128, NE])
    ident_in = din("ident_in", [128, 128], F32)
    wsh_gu_in = din("wsh_gu_in", [8, 128, NC, 256])
    wsh_d_in = din("wsh_d_in", [4, 128, 8, 2, 256])
    wexp_gu_in = din("wexp_gu_in", [NE, 8, 128, NC, 256])
    wexp_d_in = din("wexp_d_in", [NE, 4, 128, 8, 2, 256])

    out_ap = nc.dram_tensor("out", [128, NC, TOK], F32, kind="ExternalOutput").ap()

    d_delta = nc.dram_tensor("d_delta", [2, 128, T], F32).ap()
    d_du = nc.dram_tensor("d_du", [2, 128, T], BF16).ap()
    d_uD = nc.dram_tensor("d_uD", [2, 128, T], F32).ap()
    d_B = nc.dram_tensor("d_B", [NS, T], BF16).ap()
    d_C = nc.dram_tensor("d_C", [NS, T], BF16).ap()
    d_comb = nc.dram_tensor("d_comb", [NE, TOK], BF16).ap()
    ag_in = nc.dram_tensor("ag_in", [NB, 2, 128, TB], F32)
    ag_out = nc.dram_tensor("ag_out", [NB, NC, 128, TB], F32)

    dbg_aps = {}
    if "xn" in dbg:
        dbg_aps["dbg_xn"] = nc.dram_tensor("dbg_xn", [128, NC, TOK], BF16, kind="ExternalOutput").ap()
    if "comb" in dbg:
        dbg_aps["dbg_comb"] = nc.dram_tensor("dbg_comb", [NE, TOK], BF16, kind="ExternalOutput").ap()
    if "pre" in dbg:
        dbg_aps["dbg_delta"] = nc.dram_tensor("dbg_delta", [2, 128, T], F32, kind="ExternalOutput").ap()
        dbg_aps["dbg_du"] = nc.dram_tensor("dbg_du", [2, 128, T], BF16, kind="ExternalOutput").ap()
        dbg_aps["dbg_uD"] = nc.dram_tensor("dbg_uD", [2, 128, T], F32, kind="ExternalOutput").ap()
        dbg_aps["dbg_B"] = nc.dram_tensor("dbg_B", [NS, T], BF16, kind="ExternalOutput").ap()
        dbg_aps["dbg_C"] = nc.dram_tensor("dbg_C", [NS, T], BF16, kind="ExternalOutput").ap()
    if "m" in dbg:
        dbg_aps["dbg_m"] = nc.dram_tensor("dbg_m", [2, 128, T], F32, kind="ExternalOutput").ap()
        dbg_aps["dbg_msb"] = nc.dram_tensor("dbg_msb", [128, NC, TOK], F32, kind="ExternalOutput").ap()

    with ExitStack() as top:
        e = top.enter_context

        tc = Ctr(e(nc.semaphore("tc")))
        vc = Ctr(e(nc.semaphore("vc")))
        sc = Ctr(e(nc.semaphore("sc")))
        gc = Ctr(e(nc.semaphore("gc")))
        ccs = Ctr(e(nc.semaphore("ccs")))
        NQSEM = 4
        qsems = {qn: [Ctr(e(nc.semaphore(f"q{qn}{i}"))) for i in range(NQSEM)]
                 for qn in ("Q", "S", "G")}
        qidx = {"Q": 0, "S": 0, "G": 0}

        xn_own = e(nc.sbuf_tensor("xn_own", [128, NC, TOK], BF16))
        out_acc = e(nc.sbuf_tensor("out_acc", [128, NC, TOK], F32))
        comb_fm = e(nc.sbuf_tensor("comb_fm", [NE, TOK], BF16))
        h_state = e(nc.sbuf_tensor("h_state", [128, 2, NS], F32))
        A_pos = e(nc.sbuf_tensor("A_pos", [128, 2, NS], F32))
        wring = [e(nc.sbuf_tensor(f"wring{i}", [128, 4096], BF16)) for i in range(2)]
        silu_t = e(nc.sbuf_tensor("silu_t0", [128, 2, TOK], BF16))
        comb_bc = e(nc.sbuf_tensor("comb_bc", [128, TOK], BF16))
        h_sc = e(nc.sbuf_tensor("h_sc", [128, 8, TOK], BF16))
        ff_ps4 = e(nc.psum_tensor("ff_ps4", [128, 4, 512], F32))

        kb = KB()
        block = e(nc.Block())
        E = {}

        def dma(qn, mk, waits=()):
            s = qsems[qn][qidx[qn] % NQSEM]
            qidx[qn] += 1
            v = kb.emit(qn, mk, waits=[(s, s.v)] + list(waits), inc=s, inc_n=16)
            return (s, v)

        def allq():
            return [(s, s.v) for qn in qsems for s in qsems[qn]]

        def _pid(e_):
            if not hasattr(e_, "_kb_pid"):
                e_._kb_pid = e_.partition_id()
            return e_._kb_pid

        ffstate = {"ext": None}
        ffbank = {b: None for b in range(8)}
        ffgrp = [0]
        oacc = {}

        def fps(b):
            if b < 4:
                return ff_ps4[:, b, :]
            return ffstate["ext"][:, b - 4, :]

        def fpair(b0):
            if b0 < 4:
                return ff_ps4.ap()[:, b0:b0 + 2, :].rearrange("p a b -> p (a b)")
            return ffstate["ext"].ap()[:, b0 - 4:b0 - 2, :].rearrange("p a b -> p (a b)")

        def nbanks():
            return 8 if ffstate["ext"] is not None else 4

        wblocks = []
        for u in range(17):
            gu, dn = ((wsh_gu_in, wsh_d_in) if u == 0
                      else (wexp_gu_in[u - 1], wexp_d_in[u - 1]))
            for b8 in range(8):
                wblocks.append((gu[b8].rearrange("p a b -> p (a b)"), u, "gu", b8))
            for j4 in range(4):
                wblocks.append((dn[j4].rearrange("p a b c -> p (a b c)"), u, "dn", j4))
        NWB = len(wblocks)
        widx = {(u, k, i): w for w, (_, u, k, i) in enumerate(wblocks)}

        wslot_map = {}
        wsw = [None, -1]

        wcand = set()

        def wdma(w):
            if w < NWB:
                wcand.add(w)
            for ww in sorted(wcand):
                if ("wdma", ww) in E:
                    wcand.discard(ww)
                    continue
                if wsw[0] is None or ww < wsw[0]:
                    slot = ww % 2
                    prevw = ww - 2
                else:
                    slot = (ww - wsw[0]) % 3
                    prevw = ww - 3 if (ww - wsw[0]) >= 3 else wsw[0] - 1
                if prevw >= 0 and ("ffuse", prevw) not in E:
                    break
                src = wblocks[ww][0]
                wslot_map[ww] = slot
                wsw[1] = max(wsw[1], ww)
                E[("wdma", ww)] = dma(
                    "G", lambda g, src=src, slot=slot: g.dma_start(out=wring[slot][:], in_=src),
                    [(tc, E.get(("ffuse", prevw)))])
                wcand.discard(ww)

        def ff_gen(u):
            ue = u - 1
            wdma(widx[(u, "gu", 0)])
            wdma(widx[(u, "gu", 0)] + 1)
            wdma(widx[(u, "gu", 0)] + 2)
            if u >= 1:
                E[("cbc", u)] = dma(
                    "S", lambda s, ue=ue: s.dma_start(
                        out=comb_bc[:], in_=d_comb[ue:ue + 1, :].partition_broadcast(128)),
                    [(vc, E.get(("combuse", u - 1))), (vc, E["combrdy"])])
            yield

            for i in range(4):
                pg = u * 4 + i
                for gu01 in (0, 1):
                    w = widx[(u, "gu", 2 * i + gu01)]
                    wdma(w)
                    wv = wring[wslot_map[w]].ap().rearrange("p (a b) -> p a b", a=NC)
                    nb = nbanks()
                    b0g = (ffgrp[0] % (nb // 4)) * 4
                    ffgrp[0] += 1
                    bw = [ffbank[b0g + b] for b in range(4)]

                    def mk_gu(t, wv=wv, b0g=b0g):
                        for jg01 in (0, 1):
                            for c in range(NC):
                                st = (c == 0)
                                sp = (c == NC - 1)
                                t.matmul(out=fps(b0g + jg01 * 2 + 0),
                                         lhsT=wv[:, c, jg01 * 128:(jg01 + 1) * 128],
                                         rhs=xn_own[:, c, 0:512], start=st, stop=sp)
                                bi = t.matmul(out=fps(b0g + jg01 * 2 + 1),
                                              lhsT=wv[:, c, jg01 * 128:(jg01 + 1) * 128],
                                              rhs=xn_own[:, c, 512:1024], start=st, stop=sp)
                        return bi
                    mm = kb.emit("T", mk_gu,
                                 waits=[E[("wdma", w)], (vc, E["xnrdy"])] + bw, inc=tc)
                    E[("ffuse", w)] = mm
                    E[("gumm", u, i, gu01)] = mm
                    wdma(w + 2)
                    wdma(w + 3)

                    if gu01 == 0:
                        for jg01 in (0, 1):
                            for th in (0, 1):
                                ev = kb.emit(
                                    "S", lambda s, jg01=jg01, th=th, b0g=b0g: s.activation(
                                        out=silu_t[:, jg01, th * 512:(th + 1) * 512],
                                        in_=fps(b0g + jg01 * 2 + th), func=AF.Silu),
                                    waits=[(tc, mm), (vc, E.get(("upev_b", pg - 1, 1)))],
                                    inc=sc)
                                E[("silu", u, i, jg01, th)] = ev
                                ffbank[b0g + jg01 * 2 + th] = (sc, ev)
                    else:
                        if u >= 1:
                            scev = kb.emit(
                                "V", lambda v: v.tensor_mul(
                                    out=silu_t[:], in0=silu_t[:],
                                    in1=comb_bc.ap().unsqueeze(1).to_broadcast([128, 2, TOK])),
                                waits=[(sc, E[("silu", u, i, 1, 1)]), E[("cbc", u)]],
                                inc=vc)
                            E[("combuse", u)] = scev
                        for jg01 in (0, 1):
                            ev = kb.emit(
                                "V", lambda v, i=i, jg01=jg01, b0g=b0g: v.tensor_mul(
                                    out=h_sc[:, 2 * i + jg01, :],
                                    in0=fpair(b0g + jg01 * 2),
                                    in1=silu_t[:, jg01, :]),
                                waits=[(tc, mm),
                                       ((vc, scev) if u >= 1
                                        else (sc, E[("silu", u, i, jg01, 1)])),
                                       (tc, E.get(("dnmmlast", u - 1)))], inc=vc)
                            E[("upev_b", pg, jg01)] = ev
                            ffbank[b0g + jg01 * 2 + 0] = (vc, ev)
                            ffbank[b0g + jg01 * 2 + 1] = (vc, ev)
                yield

            upw = [(vc, E[("upev_b", u * 4 + 3, jj)]) for jj in (0, 1)]

            for j4 in range(4):
                w = widx[(u, "dn", j4)]
                wdma(w)
                wv = wring[wslot_map[w]].ap().rearrange("p (a b) -> p a b", a=8)
                for m4 in range(4):
                    dchunk = 4 * j4 + m4
                    nb = nbanks()
                    bank0 = (ffgrp[0] % (nb // 2)) * 2
                    ffgrp[0] += 1
                    bw = [ffbank[bank0], ffbank[bank0 + 1]]

                    def mk_dn(t, wv=wv, m4=m4, bank0=bank0):
                        for fc in range(8):
                            st = (fc == 0)
                            sp = (fc == 7)
                            t.matmul(out=fps(bank0 + 0),
                                     lhsT=wv[:, fc, m4 * 128:(m4 + 1) * 128],
                                     rhs=h_sc[:, fc, 0:512], start=st, stop=sp)
                            bi = t.matmul(out=fps(bank0 + 1),
                                          lhsT=wv[:, fc, m4 * 128:(m4 + 1) * 128],
                                          rhs=h_sc[:, fc, 512:1024], start=st, stop=sp)
                        return bi
                    mm = kb.emit("T", mk_dn, waits=[E[("wdma", w)]] + bw + upw, inc=tc)
                    E[("ffuse", w)] = mm
                    E[("dnmm", u, j4, m4)] = mm
                    if m4 == 3:
                        wdma(w + 2)
                        wdma(w + 3)
                    def mk_dnev(v, u=u, dchunk=dchunk, bank0=bank0):
                        od = (slice(None), dchunk, slice(None))
                        if u == 0:
                            return v.tensor_copy(out=out_acc[od], in_=fpair(bank0))
                        return v.tensor_add(out=out_acc[od], in0=fpair(bank0),
                                            in1=out_acc[od])
                    ev = kb.emit("V", mk_dnev,
                                 waits=[(tc, mm), (vc, oacc.get(dchunk))], inc=vc)
                    oacc[dchunk] = ev
                    ffbank[bank0 + 0] = (vc, ev)
                    ffbank[bank0 + 1] = (vc, ev)
                    if m4 % 2 == 1:
                        yield
            E[("dnmmlast", u)] = E[("dnmm", u, 3, 3)]
            yield

        ff_units = [ff_gen(u) for u in range(17)]

        class FFPull:
            def __init__(self, gens):
                self.gens = list(gens)
                self.cur = 0
                self.budget = 0.0

            def pull(self, k):
                self.budget += k
                while self.budget >= 1.0 and self.cur < len(self.gens):
                    try:
                        next(self.gens[self.cur])
                    except StopIteration:
                        self.cur += 1
                        continue
                    self.budget -= 1.0

            def drain(self):
                while self.cur < len(self.gens):
                    try:
                        next(self.gens[self.cur])
                    except StopIteration:
                        self.cur += 1

        ffp = FFPull(ff_units)
        NQUANTA = 17.0 * 14.0

        with ExitStack() as pre:
            ep = pre.enter_context
            alog_sb = ep(nc.sbuf_tensor("alog_sb", [128, 2, NS], F32))
            onec = ep(nc.sbuf_tensor("onec", [128, 2], F32))
            bdt_sb = ep(nc.sbuf_tensor("bdt_sb", [128, 2], F32))
            dpar_sb = ep(nc.sbuf_tensor("dpar_sb", [128, 2], F32))
            ebias_sb = ep(nc.sbuf_tensor("ebias_sb", [128, NE], F32))
            gw_f = ep(nc.sbuf_tensor("gw_f", [128, NC, NE], F32))
            wbc_bf = ep(nc.sbuf_tensor("wbc_bf", [128, NC, 128], BF16))
            wdt_bf = ep(nc.sbuf_tensor("wdt_bf", [RK, 2, 128], BF16))
            ident_f = ep(nc.sbuf_tensor("ident_f", [128, 128], F32))
            ones_bcol = ep(nc.sbuf_tensor("ones_bcol", [128, 1], BF16))
            ones_f = ep(nc.sbuf_tensor("ones_f", [1, 128], F32))
            eps_sb = ep(nc.sbuf_tensor("eps_sb", [1, 1], F32))
            x_f = [ep(nc.sbuf_tensor(f"x_f{i}", [128, NC, PC], F32)) for i in range(2)]
            x_bf = ep(nc.sbuf_tensor("x_bf0", [128, NC, PC], BF16))
            xn_tmp = x_bf
            mean_sb = ep(nc.sbuf_tensor("mean_sb", [1, 2, PC], F32))
            rstd_sb = ep(nc.sbuf_tensor("rstd_sb", [1, PC], F32))
            dtr_bf = [ep(nc.sbuf_tensor("dtr_bf0", [RK, PC], BF16))] * 2
            st_B = [ep(nc.sbuf_tensor("st_B0", [NS, PC], BF16))] * 2
            st_C = [ep(nc.sbuf_tensor("st_C0", [NS, PC], BF16))] * 2
            sp_t = ep(nc.sbuf_tensor("sp_t", [128, 2, PC], F32))
            x_sl = sp_t
            dneg = [ep(nc.sbuf_tensor("dneg0", [128, 2, PC], F32))] * 2
            duneg = [ep(nc.sbuf_tensor("duneg0", [128, 2, PC], F32))] * 2
            uDc = ep(nc.sbuf_tensor("uDc0", [128, 2, PC], F32))
            u_c = [ep(nc.sbuf_tensor("u_c0", [128, 2, PC], F32))] * 2
            c_sc = ep(nc.sbuf_tensor("c_sc", [128, 2, NE], F32))
            c_sel = ep(nc.sbuf_tensor("c_sel", [128, 2, NE], F32))
            c_eq1 = ep(nc.sbuf_tensor("c_eq1", [128, 2, NE], F32))
            c_eq2 = ep(nc.sbuf_tensor("c_eq2", [128, 2, NE], F32))
            c_m = ep(nc.sbuf_tensor("c_m", [128, 2], F32))
            c_den = ep(nc.sbuf_tensor("c_den", [128, 2], F32))
            c_out = ep(nc.sbuf_tensor("c_out", [128, 2, NE], F32))
            pb_stat = ep(nc.psum_tensor("pb_stat", [1, 2, PC], F32))
            pb_ln = ep(nc.psum_tensor("pb_ln", [128, 2, PC], F32))
            pb_dbc = ep(nc.psum_tensor("pb_dbc", [128, PC], F32))
            pb_misc = ep(nc.psum_tensor("pb_misc", [128, 512], F32))

            E["ld_wbc"] = dma("G", lambda g: g.dma_start(out=wbc_bf[:], in_=wbc_in))
            E["ld_wdt"] = dma("G", lambda g: g.dma_start(out=wdt_bf[:], in_=wdt_in))
            E["ld_gwf"] = dma("Q", lambda q: q.dma_start(out=gw_f[:], in_=gw_in))
            E["ld_id"] = dma("G", lambda g: g.dma_start(out=ident_f[:], in_=ident_in))
            E["ld_bdt"] = dma("Q", lambda q: q.dma_start(out=bdt_sb[:], in_=bdt_in))
            E["ld_alog"] = dma("Q", lambda q: q.dma_start(out=alog_sb[:], in_=alog_in))
            E["ld_dpar"] = dma("Q", lambda q: q.dma_start(out=dpar_sb[:], in_=dpar_in))
            E["ld_ebias"] = dma("Q", lambda q: q.dma_start(out=ebias_sb[:], in_=ebias_in))

            kb.emit("V", lambda v: v.memset(ones_bcol[:], float(np.float32(1.0 / D))))
            kb.emit("V", lambda v: v.memset(ones_f[:], 1.0))
            E["veps"] = kb.emit("V", lambda v: v.memset(eps_sb[:], 1e-5), inc=vc)
            E["onec"] = kb.emit("V", lambda v: v.memset(onec[:], 1.0), inc=vc)
            E["hzero"] = kb.emit("V", lambda v: v.memset(h_state[:], 0.0), inc=vc)
            E["apos0"] = kb.emit("S", lambda s: s.activation(out=A_pos[:], in_=alog_sb[:], func=AF.Exp),
                                 waits=[E["ld_alog"]], inc=sc)
            E["apos"] = kb.emit("V", lambda v: v.tensor_scalar_mul(out=A_pos[:], in0=A_pos[:], scalar1=-1.0),
                                waits=[(sc, E["apos0"])], inc=vc)

            ITSEQ = list(range(NPC, NPC + 4)) + list(range(NPC))

            def stageA(pos):
                it = ITSEQ[pos]
                itp1 = ITSEQ[pos - 1] if pos >= 1 else None
                itp2 = ITSEQ[pos - 2] if pos >= 2 else None
                own = it >= NPC
                r = pos % 2
                if own:
                    j = it - NPC

                    def src_ap(e_, j=j):
                        return x_own_in[j]
                else:
                    def src_ap(e_, it=it):
                        return x_in[it]

                E[("xf", it)] = dma(
                    "Q", lambda q, sa=src_ap, r=r: q.dma_start(out=x_f[r][:], in_=sa(q)),
                    [(vc, E.get(("xfdone", itp2))), (tc, E.get(("rt", itp2)))])
                E[("xbf", it)] = dma(
                    "G", lambda g, sa=src_ap: g.dma_start(out=x_bf[:], in_=sa(g)),
                    [(tc, E.get(("stats", itp1))), (tc, E.get(("dbc", itp1)))])
                def mk_statsA(t):
                    for c in range(NC):
                        bi = t.matmul(out=pb_stat[:, 0, :], lhsT=ones_bcol[:], rhs=x_bf[:, c, :],
                                      start=(c == 0), stop=(c == NC - 1))
                    return bi
                E[("statsA", it)] = kb.emit(
                    "T", mk_statsA,
                    waits=[E[("xbf", it)], (vc, E["hzero"]),
                           (vc, E.get(("rows0", itp1)))], inc=tc)
                E[("xsq", it)] = kb.emit(
                    "S", lambda s, r=r: s.activation(out=x_bf[:], in_=x_f[r][:], func=AF.Square),
                    waits=[E[("xf", it)], (tc, E[("statsA", it)])], inc=sc)

                def mk_statsB(t):
                    for c in range(NC):
                        bi = t.matmul(out=pb_stat[:, 1, :], lhsT=ones_bcol[:], rhs=x_bf[:, c, :],
                                      start=(c == 0), stop=(c == NC - 1))
                    return bi
                E[("stats", it)] = kb.emit(
                    "T", mk_statsB, waits=[(sc, E[("xsq", it)])], inc=tc)

                def mk_rows0(v):
                    v.tensor_copy(out=mean_sb[:, 0, :], in_=pb_stat[:, 0, :])
                    return v.tensor_copy(out=mean_sb[:, 1, :], in_=pb_stat[:, 1, :])
                E[("rows0", it)] = kb.emit(
                    "V", mk_rows0,
                    waits=[(tc, E[("stats", it)]), (vc, E.get(("rows3", itp1))),
                           (tc, E.get(("lnbc", itp1)))], inc=vc)
                E[("rows1", it)] = kb.emit(
                    "V", lambda v: v.tensor_mul(out=rstd_sb[:], in0=mean_sb[:, 0, :], in1=mean_sb[:, 0, :]),
                    waits=[(vc, E[("rows0", it)])], inc=vc)
                E[("rows2", it)] = kb.emit(
                    "V", lambda v: v.tensor_sub(out=mean_sb[:, 1, :], in0=mean_sb[:, 1, :], in1=rstd_sb[:]),
                    waits=[(vc, E[("rows1", it)])], inc=vc)
                E[("rows3a", it)] = kb.emit(
                    "S", lambda s: s.activation(out=mean_sb[:, 1, :], in_=mean_sb[:, 1, :],
                                                func=AF.Sqrt, bias=eps_sb[:]),
                    waits=[(vc, E[("rows2", it)]), (vc, E["veps"])], inc=sc)
                E[("rows3", it)] = kb.emit(
                    "V", lambda v: v.reciprocal(out=mean_sb[:, 1, :], in_=mean_sb[:, 1, :]),
                    waits=[(sc, E[("rows3a", it)])], inc=vc)

                def mk_bcast(t):
                    t.matmul(out=pb_ln[:, 0, :], lhsT=ones_f[:], rhs=mean_sb[0:1, 0, :],
                             start=True, stop=True)
                    return t.matmul(out=pb_ln[:, 1, :], lhsT=ones_f[:], rhs=mean_sb[0:1, 1, :],
                                    start=True, stop=True)
                E[("lnbc", it)] = kb.emit(
                    "T", mk_bcast,
                    waits=[(vc, E[("rows3", it)]), (vc, E.get(("ub", itp1))),
                           (vc, E.get(("xfdone", itp1))), (vc, E["hzero"])], inc=tc)

            def stageB1(pos):
                it = ITSEQ[pos]
                itp1 = ITSEQ[pos - 1] if pos >= 1 else None
                itp2 = ITSEQ[pos - 2] if pos >= 2 else None
                own = it >= NPC
                r = pos % 2
                j = it - NPC

                E[("xc", it)] = kb.emit(
                    "V", lambda v, r=r: v.tensor_sub(
                        out=x_f[r][:], in0=x_f[r][:],
                        in1=pb_ln.ap()[:, 0:1, :].to_broadcast([128, NC, PC])),
                    waits=[(tc, E[("lnbc", it)]), (sc, E[("xsq", it)])], inc=vc)

                if own:
                    E[("xn2", it)] = kb.emit(
                        "V", lambda v, r=r: v.tensor_mul(
                            out=x_f[r][:], in0=x_f[r][:],
                            in1=pb_ln.ap()[:, 1:2, :].to_broadcast([128, NC, PC])),
                        waits=[(vc, E[("xc", it)])], inc=vc)
                    E[("xncast", it)] = kb.emit(
                        "V", lambda v, j=j, r=r: v.tensor_copy(
                            out=xn_own[:, :, j * PC:(j + 1) * PC], in_=x_f[r][:]),
                        waits=[(vc, E[("xn2", it)])], inc=vc)
                    E[("xfdone", it)] = E[("xncast", it)]
                    if j == 3:
                        E["xnrdy"] = E[("xncast", it)]
                else:
                    E[("xn", it)] = kb.emit(
                        "V", lambda v, r=r: v.tensor_mul(
                            out=xn_tmp[:], in0=x_f[r][:],
                            in1=pb_ln.ap()[:, 1:2, :].to_broadcast([128, NC, PC])),
                        waits=[(vc, E[("xc", it)]), (tc, E[("stats", it)]),
                               (tc, E.get(("dbc", itp1)))], inc=vc)
                    E[("xfdone", it)] = E[("xn", it)]

                if not own:
                    E[("xsl", it)] = dma(
                        "Q", lambda q, it=it: q.dma_start(out=x_sl[:], in_=x_sl_in[it]),
                        [(vc, E.get(("ub", itp1))), (vc, E.get(("dsum", itp1)))])
                    E[("ua", it)] = kb.emit(
                        "V", lambda v: v.tensor_sub(
                            out=x_sl[:], in0=x_sl[:],
                            in1=pb_ln.ap()[:, 0:1, :].to_broadcast([128, 2, PC])),
                        waits=[E[("xsl", it)], (tc, E[("lnbc", it)])], inc=vc)
                    E[("ub", it)] = kb.emit(
                        "V", lambda v, r=r: v.tensor_mul(
                            out=u_c[r][:], in0=x_sl[:],
                            in1=pb_ln.ap()[:, 1:2, :].to_broadcast([128, 2, PC])),
                        waits=[(vc, E[("ua", it)]), (gc, E.get(("uD", itp1))),
                               (gc, E.get(("du", itp1)))], inc=vc)

                    def mk_dbc(t):
                        for c in range(NC):
                            bi = t.matmul(out=pb_dbc[:], lhsT=wbc_bf[:, c, :], rhs=xn_tmp[:, c, :],
                                          start=(c == 0), stop=(c == NC - 1))
                        return bi
                    E[("dbc", it)] = kb.emit(
                        "T", mk_dbc,
                        waits=[(vc, E[("xn", it)]), E["ld_wbc"],
                               (vc, E.get(("dbcev", itp1)))], inc=tc)

                    def mk_dbcev(v, r=r):
                        v.tensor_copy(out=dtr_bf[r][:], in_=pb_dbc.ap()[0:RK, :])
                        v.tensor_copy(out=st_B[r][:], in_=pb_dbc.ap()[RK:RK + NS, :])
                        return v.tensor_copy(out=st_C[r][:], in_=pb_dbc.ap()[RK + NS:RK + 2 * NS, :])
                    E[("dbcev", it)] = kb.emit(
                        "V", mk_dbcev,
                        waits=[(tc, E[("dbc", it)]), (tc, E.get(("dt", itp1))),
                               E.get(("wBd", itp1)), E.get(("wCd", itp1))], inc=vc)

            def stageB2(pos):
                it = ITSEQ[pos]
                itp1 = ITSEQ[pos - 1] if pos >= 1 else None
                itp2 = ITSEQ[pos - 2] if pos >= 2 else None
                own = it >= NPC
                r = pos % 2
                j = it - NPC

                if not own:
                    E[("wBd", it)] = dma(
                        "Q", lambda q, it=it, r=r: q.dma_start(
                            out=d_B[:, it * PC:(it + 1) * PC], in_=st_B[r][:]),
                        [(vc, E[("dbcev", it)])])
                    E[("wCd", it)] = dma(
                        "Q", lambda q, it=it, r=r: q.dma_start(
                            out=d_C[:, it * PC:(it + 1) * PC], in_=st_C[r][:]),
                        [(vc, E[("dbcev", it)])])

                    def mk_dt(t, r=r):
                        t.matmul(out=pb_misc[:, 0:PC], lhsT=wdt_bf[:, 0, :], rhs=dtr_bf[r][:],
                                 start=True, stop=True)
                        return t.matmul(out=pb_misc[:, PC:2 * PC], lhsT=wdt_bf[:, 1, :], rhs=dtr_bf[r][:],
                                        start=True, stop=True)
                    E[("dt", it)] = kb.emit(
                        "T", mk_dt,
                        waits=[(vc, E[("dbcev", it)]), E["ld_wdt"], (vc, E.get(("cfm", itp1))),
                               (sc, E.get(("sp4", itp1)))], inc=tc)

                    pmv = pb_misc.ap()[:, 0:2 * PC].rearrange("p (a b) -> p a b", a=2)

                    def mk_sp1(s, pmv=pmv):
                        s.activation(out=sp_t[:, 0, :], in_=pmv[:, 0, :], func=AF.Abs,
                                     bias=bdt_sb[:, 0:1])
                        return s.activation(out=sp_t[:, 1, :], in_=pmv[:, 1, :], func=AF.Abs,
                                            bias=bdt_sb[:, 1:2])
                    E[("sp1", it)] = kb.emit(
                        "S", mk_sp1,
                        waits=[(tc, E[("dt", it)]), E["ld_bdt"], (vc, E[("ub", it)]),
                               (sc, E.get(("sp3", itp1))), (vc, E.get(("dsum", itp1)))], inc=sc)
                    E[("sp2", it)] = kb.emit(
                        "S", lambda s: s.activation(out=sp_t[:], in_=sp_t[:], func=AF.Exp, scale=-1.0),
                        waits=[(sc, E[("sp1", it)])], inc=sc)
                    E[("sp3", it)] = kb.emit(
                        "S", lambda s: s.activation(out=sp_t[:], in_=sp_t[:], func=AF.Ln, bias=onec[:, 0:1]),
                        waits=[(sc, E[("sp2", it)]), (vc, E["onec"])], inc=sc)

                    def mk_sp4(s, r=r, pmv=pmv):
                        s.activation(out=dneg[r][:, 0, :], in_=pmv[:, 0, :], func=AF.Relu,
                                     bias=bdt_sb[:, 0:1])
                        return s.activation(out=dneg[r][:, 1, :], in_=pmv[:, 1, :], func=AF.Relu,
                                            bias=bdt_sb[:, 1:2])
                    E[("sp4", it)] = kb.emit(
                        "S", mk_sp4,
                        waits=[(sc, E[("sp3", it)]), E.get(("wdn", itp1)),
                               (gc, E.get(("du", itp1)))], inc=sc)
                    E[("dsum", it)] = kb.emit(
                        "V", lambda v, r=r: v.tensor_add(out=dneg[r][:], in0=dneg[r][:], in1=sp_t[:]),
                        waits=[(sc, E[("sp4", it)])], inc=vc)

                    E[("du", it)] = kb.emit(
                        "G", lambda g, r=r: g.tensor_mul(out=duneg[r][:], in0=dneg[r][:], in1=u_c[r][:]),
                        waits=[(vc, E[("dsum", it)]), (vc, E[("ub", it)]),
                               E.get(("wdu", itp1))], inc=gc)

                    def mk_uD(g, r=r):
                        g.tensor_scalar_mul(out=uDc[:, 0, :], in0=u_c[r][:, 0, :], scalar1=dpar_sb[:, 0:1])
                        return g.tensor_scalar_mul(out=uDc[:, 1, :], in0=u_c[r][:, 1, :],
                                                   scalar1=dpar_sb[:, 1:2])
                    E[("uD", it)] = kb.emit(
                        "G", mk_uD,
                        waits=[(vc, E[("ub", it)]), E["ld_dpar"],
                               E.get(("wuD", itp1))], inc=gc)

                    tsl = slice(it * PC, (it + 1) * PC)
                    E[("wdn", it)] = dma(
                        "S", lambda s, r=r, tsl=tsl: s.dma_start(
                            out=d_delta.rearrange("cl p t -> p cl t")[:, :, tsl], in_=dneg[r][:]),
                        [(vc, E[("dsum", it)])])
                    E[("wdu", it)] = dma(
                        "G", lambda g, r=r, tsl=tsl: g.dma_start(
                            out=d_du.rearrange("cl p t -> p cl t")[:, :, tsl], in_=duneg[r][:]),
                        [(gc, E[("du", it)])])
                    E[("wuD", it)] = dma(
                        "S", lambda s, tsl=tsl: s.dma_start(
                            out=d_uD.rearrange("cl p t -> p cl t")[:, :, tsl], in_=uDc[:]),
                        [(gc, E[("uD", it)])])
                else:
                    def mk_rt(t, r=r):
                        for tt in range(2):
                            for c in range(NC):
                                bi = t.matmul(
                                    out=pb_misc[:, tt * NE:(tt + 1) * NE],
                                    lhsT=x_f[r][:, c, tt * 128:(tt + 1) * 128],
                                    rhs=gw_f[:, c, :],
                                    start=(c == 0), stop=(c == NC - 1))
                        return bi
                    E[("rt", it)] = kb.emit(
                        "T", mk_rt,
                        waits=[(vc, E[("xn2", it)]), (sc, E.get(("csc", itp1))),
                               (vc, E.get(("cfm", itp1))), E["ld_gwf"]], inc=tc)

                    E[("csc", it)] = kb.emit(
                        "S", lambda s: s.activation(
                            out=c_sc[:],
                            in_=pb_misc.ap()[:, 0:2 * NE].rearrange("p (a b) -> p a b", a=2),
                            func=AF.Sigmoid),
                        waits=[(tc, E[("rt", it)]), (vc, E.get(("cw", itp1)))], inc=sc)

                    def vch(mk, extra=()):
                        return kb.emit("V", mk, waits=[(vc, vc.v)] + list(extra), inc=vc)

                    vch(lambda v: v.tensor_add(
                        out=c_sel[:], in0=c_sc[:],
                        in1=ebias_sb.ap().unsqueeze(1).to_broadcast([128, 2, NE])),
                        extra=[(sc, E[("csc", it)]), (vc, E.get(("cfm", itp1))),
                               (tc, E.get(("ctr", itp1))), E["ld_ebias"]])
                    vch(lambda v: v.tensor_reduce(out=c_m[:], in_=c_sel[:], axis=AX.X, op=ALU.max))

                    def mk_eq1(v):
                        v.tensor_scalar(out=c_eq1[:, 0, :], in0=c_sel[:, 0, :], scalar1=c_m[:, 0:1],
                                        scalar2=None, op0=ALU.is_ge)
                        return v.tensor_scalar(out=c_eq1[:, 1, :], in0=c_sel[:, 1, :],
                                               scalar1=c_m[:, 1:2], scalar2=None, op0=ALU.is_ge)
                    vch(mk_eq1)
                    vch(lambda v: v.scalar_tensor_tensor(out=c_sel[:], in0=c_eq1[:], scalar=-1e9,
                                                         in1=c_sel[:], op0=ALU.mult, op1=ALU.add))
                    vch(lambda v: v.tensor_reduce(out=c_m[:], in_=c_sel[:], axis=AX.X, op=ALU.max))

                    def mk_eq2(v):
                        v.tensor_scalar(out=c_eq2[:, 0, :], in0=c_sel[:, 0, :], scalar1=c_m[:, 0:1],
                                        scalar2=None, op0=ALU.is_ge)
                        return v.tensor_scalar(out=c_eq2[:, 1, :], in0=c_sel[:, 1, :],
                                               scalar1=c_m[:, 1:2], scalar2=None, op0=ALU.is_ge)
                    vch(mk_eq2)
                    vch(lambda v: v.tensor_add(out=c_eq1[:], in0=c_eq1[:], in1=c_eq2[:]))
                    vch(lambda v: v.tensor_mul(out=c_eq2[:], in0=c_sc[:], in1=c_eq1[:]))
                    vch(lambda v: v.tensor_reduce(out=c_den[:], in_=c_eq2[:], axis=AX.X, op=ALU.add))
                    vch(lambda v: v.tensor_scalar_add(out=c_den[:], in0=c_den[:], scalar1=1e-20))
                    vch(lambda v: v.reciprocal(out=c_den[:], in_=c_den[:]))

                    def mk_cw(v):
                        v.tensor_scalar(out=c_out[:, 0, :], in0=c_eq2[:, 0, :], scalar1=c_den[:, 0:1],
                                        scalar2=None, op0=ALU.mult)
                        return v.tensor_scalar(out=c_out[:, 1, :], in0=c_eq2[:, 1, :],
                                               scalar1=c_den[:, 1:2], scalar2=None, op0=ALU.mult)
                    E[("cw", it)] = vch(mk_cw)

                    def mk_ctr(t):
                        t.transpose(out=pb_misc.ap()[0:NE, 64:192], in_=c_out[:, 0, :],
                                    identity=ident_f[:])
                        return t.transpose(out=pb_misc.ap()[0:NE, 192:320], in_=c_out[:, 1, :],
                                           identity=ident_f[:])
                    E[("ctr", it)] = kb.emit(
                        "T", mk_ctr,
                        waits=[(vc, E[("cw", it)]), E["ld_id"], (sc, E[("csc", it)]),
                               (vc, E.get(("cfm", itp1)))], inc=tc)

                    def mk_cfm(v, j=j):
                        v.tensor_copy(out=comb_fm[:, j * PC: j * PC + 128],
                                      in_=pb_misc.ap()[0:NE, 64:192])
                        return v.tensor_copy(out=comb_fm[:, j * PC + 128: (j + 1) * PC],
                                             in_=pb_misc.ap()[0:NE, 192:320])
                    E[("cfm", it)] = kb.emit("V", mk_cfm, waits=[(tc, E[("ctr", it)])], inc=vc)
                    E[("wcomb", it)] = dma(
                        "Q", lambda q, j=j: q.dma_start(
                            out=d_comb[:, j * PC:(j + 1) * PC],
                            in_=comb_fm[:, j * PC:(j + 1) * PC]),
                        [(vc, E[("cfm", it)])])
                    if j == 3:
                        E["combrdy"] = E[("cfm", it)]

            NIT = NPC + 4
            for pos in range(NIT + 1):
                if pos >= 6:
                    ffp.pull(2.5)
                if pos >= 1:
                    stageB1(pos - 1)
                if pos < NIT:
                    stageA(pos)
                if pos >= 1:
                    stageB2(pos - 1)

            BAR = [(tc, tc.v), (vc, vc.v), (sc, sc.v), (gc, gc.v)] + allq()
            kb.replay(block)

        if "xn" in dbg:
            dma("Q", lambda q: q.dma_start(out=dbg_aps["dbg_xn"], in_=xn_own[:]), BAR)
        if "comb" in dbg:
            dma("Q", lambda q: q.dma_start(out=dbg_aps["dbg_comb"], in_=comb_fm[:]), BAR)
        if "pre" in dbg:
            dma("Q", lambda q: q.dma_start(out=dbg_aps["dbg_delta"], in_=d_delta), BAR)
            dma("Q", lambda q: q.dma_start(out=dbg_aps["dbg_du"], in_=d_du), BAR)
            dma("Q", lambda q: q.dma_start(out=dbg_aps["dbg_uD"], in_=d_uD), BAR)
            dma("Q", lambda q: q.dma_start(out=dbg_aps["dbg_B"], in_=d_B), BAR)
            dma("Q", lambda q: q.dma_start(out=dbg_aps["dbg_C"], in_=d_C), BAR)

        if upto == "pre":
            ffp.drain()
            fv = dma("Q", lambda q: q.dma_start(out=out_ap, in_=out_acc[:]),
                     BAR + [(tc, tc.v), (vc, vc.v), (sc, sc.v)])
            kb.emit("Q", lambda q: None, waits=[fv])
            kb.replay(block)
            return nc

        with ExitStack() as main:
            em = main.enter_context
            SRING = 3
            dblk = em(nc.sbuf_tensor("dblk", [128, 2, TB], F32))
            dublk = em(nc.sbuf_tensor("dublk", [128, 2, TB], BF16))
            yacc = em(nc.sbuf_tensor("yacc", [128, 2, TB], F32))
            dA = [em(nc.sbuf_tensor(f"dA{i}", [128, TB], F32)) for i in range(SRING)]
            dBu = [em(nc.sbuf_tensor(f"dBu{i}", [128, TB], F32)) for i in range(SRING)]
            hser = em(nc.sbuf_tensor("hser", [128, 2, TB], BF16))
            Bbc = [em(nc.sbuf_tensor(f"Bbc{i}", [128, TB], BF16)) for i in range(2)]
            Cbc = [em(nc.sbuf_tensor(f"Cbc{i}", [128, TB], BF16)) for i in range(2)]
            ffstate["ext"] = em(nc.psum_tensor("ff_ps_ext", [128, 4, 512], F32))
            wring.append(em(nc.sbuf_tensor("wring2", [128, 4096], BF16)))
            wsw[0] = wsw[1] + 1
            for eng in ENGS:
                kb.emit(eng, lambda e_: None, waits=BAR)

            def emit_gymul(b, bn):
                n = bn % NS
                npp = bn % 2
                E[("hst", bn)] = kb.emit(
                    "G", lambda g, n=n: g.tensor_copy(
                        out=h_state[:, :, n:n + 1], in_=hser[:, :, TB - 1:TB]),
                    waits=[(vc, E[("scan", bn * 2 + 1)])], inc=gc)
                E[("ymul_g", bn)] = kb.emit(
                    "G", lambda g, npp=npp: g.tensor_mul(
                        out=hser[:], in0=hser[:],
                        in1=Cbc[npp].ap().unsqueeze(1).to_broadcast([128, 2, TB])),
                    waits=[(gc, E[("hst", bn)]), E[("bcC", bn)]], inc=gc)

            def emit_yadds(b, bn):
                first = (bn % NS) == 0

                def mk_y(v, first=first):
                    if first:
                        return v.tensor_copy(out=yacc[:], in_=hser[:])
                    return v.tensor_add(out=yacc[:], in0=yacc[:], in1=hser[:])
                E[("yadd", bn)] = kb.emit(
                    "V", mk_y,
                    waits=[(gc, E[("ymul_g", bn)]), (vc, E.get(("yadd", bn - 1))),
                           E.get(("wag", b - 1)) if first else None], inc=vc)

            def sc_block(b):
                t0 = b * TB
                last_it = 4 * b + 3
                E[("dblk", b)] = dma(
                    "Q", lambda q, t0=t0: q.dma_start(
                        out=dblk[:], in_=d_delta.rearrange("cl p t -> p cl t")[:, :, t0:t0 + TB]),
                    [E[("wdn", last_it)], (sc, E.get(("dAlast", b - 1)))])
                E[("dublk", b)] = dma(
                    "Q", lambda q, t0=t0: q.dma_start(
                        out=dublk[:], in_=d_du.rearrange("cl p t -> p cl t")[:, :, t0:t0 + TB]),
                    [E[("wdu", last_it)], (gc, E.get(("dBulast", b - 1)))])

                for n in range(NS):
                    bn = b * NS + n
                    npp = bn % 2

                    E[("bcB", bn)] = dma(
                        "S", lambda s, n=n, t0=t0, npp=npp: s.dma_start(
                            out=Bbc[npp][:], in_=d_B[n:n + 1, t0:t0 + TB].partition_broadcast(128)),
                        [(gc, E.get(("dBu_g", bn - 2, 1))), E[("wBd", last_it)]])
                    E[("bcC", bn)] = dma(
                        "S", lambda s, n=n, t0=t0, npp=npp: s.dma_start(
                            out=Cbc[npp][:], in_=d_C[n:n + 1, t0:t0 + TB].partition_broadcast(128)),
                        [(gc, E.get(("ymul_g", bn - 2))), E[("wCd", last_it)]])

                    ffp.pull((NQUANTA - 36.0) / 256.0)

                    if n >= 1:
                        emit_gymul(b, bn - 1)
                        emit_yadds(b, bn - 1)

                    for c in (0, 1):
                        gi = bn * 2 + c
                        gp = gi % SRING
                        E[("dA", gi)] = kb.emit(
                            "S", lambda s, c=c, n=n, gp=gp: s.activation(
                                out=dA[gp][:], in_=dblk[:, c, :], func=AF.Exp,
                                scale=A_pos[:, c, n:n + 1]),
                            waits=[E[("dblk", b)], (vc, E["apos"]),
                                   (vc, E.get(("scan", gi - SRING)))], inc=sc)
                        E[("dBu_g", bn, c)] = kb.emit(
                            "G", lambda g, c=c, npp=npp, gp=gp: g.tensor_mul(
                                out=dBu[gp][:], in0=dublk[:, c, :], in1=Bbc[npp][:]),
                            waits=[E[("dublk", b)], E[("bcB", bn)],
                                   (vc, E.get(("scan", gi - SRING)))], inc=gc)
                        E[("scan", gi)] = kb.emit(
                            "V", lambda v, c=c, n=n, gp=gp: v.tensor_tensor_scan(
                                out=hser[:, c, :], data0=dA[gp][:], data1=dBu[gp][:],
                                initial=h_state[:, c, n:n + 1], op0=ALU.mult, op1=ALU.add),
                            waits=[(sc, E[("dA", gi)]), (gc, E[("dBu_g", bn, c)]),
                                   (gc, E.get(("hst", bn - NS))),
                                   (vc, E.get(("yadd", bn - 1))), (vc, E["hzero"])],
                            inc=vc)

                emit_gymul(b, b * NS + NS - 1)
                emit_yadds(b, b * NS + NS - 1)
                E[("dAlast", b)] = E[("dA", (b * NS + NS - 1) * 2 + 1)]
                E[("dBulast", b)] = E[("dBu_g", b * NS + NS - 1, 1)]
                E[("wag", b)] = dma(
                    "S", lambda s, b=b: s.dma_start(
                        out=ag_in.ap()[b].rearrange("cl p t -> p cl t"), in_=yacc[:]),
                    [(vc, E[("yadd", b * NS + NS - 1)])])
                E[("uacc", b)] = dma(
                    "G", lambda g, b=b, t0=t0: g.dma_start(
                        out=ag_in.ap()[b], in_=d_uD[:, :, t0:t0 + TB], accum_op=ALU.add),
                    [E[("wag", b)], E[("wuD", last_it)]])
                E[("coll", b)] = kb.emit(
                    "G", lambda g, b=b: g.collective_compute(
                        "AllGather", ALU.bypass, replica_groups=[list(range(NCORE))],
                        ins=[ag_in.ap()[b].opt()], outs=[ag_out.ap()[b].opt()]),
                    waits=[E[("uacc", b)]], inc=ccs)

            for b in range(NB):
                sc_block(b)
            ffp.drain()

            MAIN = [(tc, tc.v), (vc, vc.v), (sc, sc.v), (gc, gc.v)] + allq()
            kb.replay(block)

        with ExitStack() as post:
            epo = post.enter_context
            m_sb = epo(nc.sbuf_tensor("m_sb", [128, NC, TOK], F32))
            x_re = m_sb

            def mk_xre(q):
                return q.dma_start(
                    out=x_re.ap().rearrange("p c (a b) -> p c a b", a=4),
                    in_=x_own_in.rearrange("a p c b -> p c a b"))
            E["xre"] = dma("Q", mk_xre, MAIN)
            E["addx"] = kb.emit(
                "V", lambda v: v.tensor_add(
                    out=out_acc.ap().rearrange("p c t -> p (c t)"),
                    in0=out_acc.ap().rearrange("p c t -> p (c t)"),
                    in1=x_re.ap().rearrange("p c t -> p (c t)")),
                waits=[E["xre"]] + MAIN, inc=vc)

            def mk_mload(q):
                rk = _pid(q)
                return q.dma_start(
                    out=m_sb[:],
                    in_=ag_out.ap().rearrange("b c p t -> p (b c) t")[:, ds(rk * NC, NC), :])
            E["mload"] = dma("Q", mk_mload,
                             [(ccs, E[("coll", NB - 1)]), (vc, E["addx"])] + MAIN)

            if "m" in dbg:
                dma("Q", lambda q: q.dma_start(
                    out=dbg_aps["dbg_m"], in_=ag_in.ap().rearrange("b cl p t -> cl p (b t)")),
                    [(ccs, E[("coll", NB - 1)])])
                E["msbd"] = dma("Q", lambda q: q.dma_start(out=dbg_aps["dbg_msb"], in_=m_sb[:]),
                                [E["mload"]])

            E["addm"] = kb.emit(
                "V", lambda v: v.tensor_add(
                    out=out_acc.ap().rearrange("p c t -> p (c t)"),
                    in0=out_acc.ap().rearrange("p c t -> p (c t)"),
                    in1=m_sb.ap().rearrange("p c t -> p (c t)")),
                waits=[E.get("msbd"), E["mload"], (vc, E["addx"])], inc=vc)

            fv = dma("Q", lambda q: q.dma_start(out=out_ap, in_=out_acc[:]),
                     [(vc, E["addm"])])
            kb.emit("Q", lambda q: None, waits=[fv])

            kb.replay(block)

    return nc


_NC_CACHE = {}


def _get_nc(upto="all", dbg=()):
    key = (upto, tuple(sorted(dbg)))
    if key not in _NC_CACHE:
        _NC_CACHE[key] = build_nc(upto, dbg)
    return _NC_CACHE[key]


def _prep_inputs(L0_attn_x, L0_attn_W_bc, L0_attn_W_dt, L0_attn_b_dt, L0_attn_A_log,
                 L0_attn_D_param, L0_ffn_gate_w, L0_ffn_expert_bias, L0_ffn_shared_gate_w,
                 L0_ffn_shared_up_w, L0_ffn_shared_down_w, L0_ffn_experts_gate_up,
                 L0_ffn_experts_down):
    f32 = np.float32
    A = np.ascontiguousarray

    x = np.asarray(L0_attn_x, f32).reshape(T, D)
    xT = A(x.T)
    x_in = A(xT.reshape(NC, 128, NPC, PC).transpose(2, 1, 0, 3))

    wbc = A(np.asarray(L0_attn_W_bc, f32).T.reshape(NC, 128, 128).transpose(1, 0, 2))
    gw = A(np.asarray(L0_ffn_gate_w, f32).reshape(NC, 128, NE).transpose(1, 0, 2))
    ebias = A(np.broadcast_to(np.asarray(L0_ffn_expert_bias, f32)[None, :], (128, NE)))
    ident = np.eye(128, dtype=f32)

    def gu_layout(w):
        a = np.asarray(w, f32).reshape(NC, 128, 8, 256).transpose(2, 1, 0, 3)
        order = [0, 4, 1, 5, 2, 6, 3, 7]
        return A(a[order])

    def dn_layout(w):
        a = np.asarray(w, f32).reshape(8, 128, 8, 256).transpose(2, 1, 0, 3)
        a = a.reshape(4, 2, 128, 8, 256).transpose(0, 2, 3, 1, 4)
        return A(a)

    sh_gu = gu_layout(np.concatenate(
        [np.asarray(L0_ffn_shared_gate_w, f32), np.asarray(L0_ffn_shared_up_w, f32)], axis=1))
    sh_d = dn_layout(np.asarray(L0_ffn_shared_down_w, f32))
    ex_gu = A(np.stack([gu_layout(np.asarray(L0_ffn_experts_gate_up, f32)[k]) for k in range(NE)]))
    ex_d = A(np.stack([dn_layout(np.asarray(L0_ffn_experts_down, f32)[k]) for k in range(NE)]))

    common = dict(x_in=x_in, wbc_in=wbc, gw_in=gw, ebias_in=ebias, ident_in=ident,
                  wsh_gu_in=sh_gu, wsh_d_in=sh_d, wexp_gu_in=ex_gu, wexp_d_in=ex_d)

    x4 = x_in.reshape(NCORE, 4, 128, NC, PC)
    Wdt = np.asarray(L0_attn_W_dt, f32)
    bdt = np.asarray(L0_attn_b_dt, f32)
    Alog = np.asarray(L0_attn_A_log, f32)
    Dpar = np.asarray(L0_attn_D_param, f32)
    in_maps = []
    for k in range(NCORE):
        sl = slice(256 * k, 256 * (k + 1))
        im = dict(common)
        im["wdt_in"] = A(Wdt[sl].T.reshape(RK, 2, 128))
        im["bdt_in"] = A(bdt[sl].reshape(2, 128).T)
        im["alog_in"] = A(Alog[sl].reshape(2, 128, NS).transpose(1, 0, 2))
        im["dpar_in"] = A(Dpar[sl].reshape(2, 128).T)
        im["x_own_in"] = A(x4[k])
        im["x_sl_in"] = A(xT[sl].reshape(2, 128, NPC, PC).transpose(2, 1, 0, 3))
        in_maps.append(im)
    return in_maps


def kernel(L0_attn_x, L0_attn_z, L0_attn_W_bc, L0_attn_W_dt, L0_attn_b_dt, L0_attn_A_log,
           L0_attn_D_param, L0_ffn_gate_w, L0_ffn_expert_bias, L0_ffn_shared_gate_w,
           L0_ffn_shared_up_w, L0_ffn_shared_down_w, L0_ffn_experts_gate_up,
           L0_ffn_experts_down):
    in_maps = _prep_inputs(
        L0_attn_x, L0_attn_W_bc, L0_attn_W_dt, L0_attn_b_dt, L0_attn_A_log,
        L0_attn_D_param, L0_ffn_gate_w, L0_ffn_expert_bias, L0_ffn_shared_gate_w,
        L0_ffn_shared_up_w, L0_ffn_shared_down_w, L0_ffn_experts_gate_up,
        L0_ffn_experts_down)
    nc = _get_nc()
    res = run_bass_kernel_spmd(nc, in_maps, core_ids=list(range(NCORE)))
    out = np.empty((1, T, D), np.float32)
    for k in range(NCORE):
        o = res.results[k]["out"]
        out[0, TOK * k:TOK * (k + 1), :] = o.transpose(2, 1, 0).reshape(TOK, D)
    return out


# version 15
# speedup vs baseline: 1.0442x; 1.0442x over previous
import numpy as np
import concourse.bass as bass
import concourse.mybir as mybir
from concourse.bass_utils import run_bass_kernel_spmd
from contextlib import ExitStack

F32 = mybir.dt.float32
BF16 = mybir.dt.bfloat16
AF = mybir.ActivationFunctionType
ALU = mybir.AluOpType
AX = mybir.AxisListType
ds = bass.ds

T = 8192
D = 2048
NC = 16
NS = 32
RK = 64
NE = 16
PC = 256
NPC = T // PC
TB = 1024
NB = T // TB
TOK = 1024
NCORE = 8

ENGS = ("T", "V", "S", "G", "Q")


class Ctr:
    def __init__(self, sem):
        self.sem = sem
        self.v = 0

    def bump(self, n):
        self.v += n
        return self.v


class KB:
    def __init__(self):
        self.prog = {e: [] for e in ENGS}
        self.lastwait = {e: {} for e in ENGS}

    def emit(self, eng, mk, waits=(), inc=None, inc_n=1):
        lw = self.lastwait[eng]
        todo = []
        for cv in waits:
            if cv is None:
                continue
            c, v = cv
            if c is None or v is None or v <= 0:
                continue
            if lw.get(id(c), -1) >= v:
                continue
            lw[id(c)] = v
            todo.append((c, v))
        val = inc.bump(inc_n) if inc is not None else None

        def fn(e, todo=tuple(todo), mk=mk, inc=inc, inc_n=inc_n):
            for c, v in todo:
                e.wait_ge(c.sem, v)
            bi = mk(e)
            if inc is not None and bi is not None:
                bi.then_inc(inc.sem, inc_n)

        self.prog[eng].append(fn)
        return val

    def replay(self, block):
        secs = {"T": block.tensor, "V": block.vector, "S": block.scalar,
                "G": block.gpsimd, "Q": block.sync}
        for e in ENGS:
            fns = self.prog[e]
            if fns:
                secs[e](lambda eng, fns=tuple(fns): [fn(eng) for fn in fns])
            self.prog[e] = []
            self.lastwait[e] = {}


def build_nc(upto="all", dbg=()):
    nc = bass.Bass("TRN2", target_bir_lowering=False)

    def din(name, shape, dt=F32):
        return nc.dram_tensor(name, shape, dt, kind="ExternalInput").ap()

    x_in = din("x_in", [NPC, 128, NC, PC])
    x_own_in = din("x_own_in", [4, 128, NC, PC])
    x_sl_in = din("x_sl_in", [NPC, 128, 2, PC])
    wbc_in = din("wbc_in", [128, NC, 128])
    wdt_in = din("wdt_in", [RK, 2, 128])
    bdt_in = din("bdt_in", [128, 2])
    alog_in = din("alog_in", [128, 2, NS])
    dpar_in = din("dpar_in", [128, 2])
    gw_in = din("gw_in", [128, NC, NE])
    ebias_in = din("ebias_in", [128, NE])
    ident_in = din("ident_in", [128, 128], F32)
    wsh_gu_in = din("wsh_gu_in", [8, 128, NC, 256])
    wsh_d_in = din("wsh_d_in", [4, 128, 8, 2, 256])
    wexp_gu_in = din("wexp_gu_in", [NE, 8, 128, NC, 256])
    wexp_d_in = din("wexp_d_in", [NE, 4, 128, 8, 2, 256])

    out_ap = nc.dram_tensor("out", [128, NC, TOK], F32, kind="ExternalOutput").ap()

    d_delta = nc.dram_tensor("d_delta", [2, 128, T], F32).ap()
    d_du = nc.dram_tensor("d_du", [2, 128, T], BF16).ap()
    d_uD = nc.dram_tensor("d_uD", [2, 128, T], F32).ap()
    d_B = nc.dram_tensor("d_B", [NS, T], BF16).ap()
    d_C = nc.dram_tensor("d_C", [NS, T], BF16).ap()
    d_comb = nc.dram_tensor("d_comb", [NE, TOK], BF16).ap()
    ag_in = nc.dram_tensor("ag_in", [NB, 2, 128, TB], F32)
    ag_out = nc.dram_tensor("ag_out", [NB, NC, 128, TB], F32)

    dbg_aps = {}
    if "xn" in dbg:
        dbg_aps["dbg_xn"] = nc.dram_tensor("dbg_xn", [128, NC, TOK], BF16, kind="ExternalOutput").ap()
    if "comb" in dbg:
        dbg_aps["dbg_comb"] = nc.dram_tensor("dbg_comb", [NE, TOK], BF16, kind="ExternalOutput").ap()
    if "pre" in dbg:
        dbg_aps["dbg_delta"] = nc.dram_tensor("dbg_delta", [2, 128, T], F32, kind="ExternalOutput").ap()
        dbg_aps["dbg_du"] = nc.dram_tensor("dbg_du", [2, 128, T], BF16, kind="ExternalOutput").ap()
        dbg_aps["dbg_uD"] = nc.dram_tensor("dbg_uD", [2, 128, T], F32, kind="ExternalOutput").ap()
        dbg_aps["dbg_B"] = nc.dram_tensor("dbg_B", [NS, T], BF16, kind="ExternalOutput").ap()
        dbg_aps["dbg_C"] = nc.dram_tensor("dbg_C", [NS, T], BF16, kind="ExternalOutput").ap()
    if "m" in dbg:
        dbg_aps["dbg_m"] = nc.dram_tensor("dbg_m", [2, 128, T], F32, kind="ExternalOutput").ap()
        dbg_aps["dbg_msb"] = nc.dram_tensor("dbg_msb", [128, NC, TOK], F32, kind="ExternalOutput").ap()

    with ExitStack() as top:
        e = top.enter_context

        tc = Ctr(e(nc.semaphore("tc")))
        vc = Ctr(e(nc.semaphore("vc")))
        sc = Ctr(e(nc.semaphore("sc")))
        gc = Ctr(e(nc.semaphore("gc")))
        ccs = Ctr(e(nc.semaphore("ccs")))
        NQSEM = 4
        qsems = {qn: [Ctr(e(nc.semaphore(f"q{qn}{i}"))) for i in range(NQSEM)]
                 for qn in ("Q", "S", "G")}
        qidx = {"Q": 0, "S": 0, "G": 0}

        xn_own = e(nc.sbuf_tensor("xn_own", [128, NC, TOK], BF16))
        out_acc = e(nc.sbuf_tensor("out_acc", [128, NC, TOK], F32))
        comb_fm = e(nc.sbuf_tensor("comb_fm", [NE, TOK], BF16))
        h_state = e(nc.sbuf_tensor("h_state", [128, 2, NS], F32))
        A_pos = e(nc.sbuf_tensor("A_pos", [128, 2, NS], F32))
        wring = [e(nc.sbuf_tensor(f"wring{i}", [128, 4096], BF16)) for i in range(2)]
        silu_t = e(nc.sbuf_tensor("silu_t0", [128, 2, TOK], BF16))
        comb_bc = e(nc.sbuf_tensor("comb_bc", [128, TOK], BF16))
        h_sc = e(nc.sbuf_tensor("h_sc", [128, 8, TOK], BF16))
        ff_ps4 = e(nc.psum_tensor("ff_ps4", [128, 4, 512], F32))

        kb = KB()
        block = e(nc.Block())
        E = {}

        def dma(qn, mk, waits=()):
            s = qsems[qn][qidx[qn] % NQSEM]
            qidx[qn] += 1
            v = kb.emit(qn, mk, waits=[(s, s.v)] + list(waits), inc=s, inc_n=16)
            return (s, v)

        def allq():
            return [(s, s.v) for qn in qsems for s in qsems[qn]]

        def _pid(e_):
            if not hasattr(e_, "_kb_pid"):
                e_._kb_pid = e_.partition_id()
            return e_._kb_pid

        ffstate = {"ext": None}
        ffbank = {b: None for b in range(8)}
        ffgrp = [0]
        oacc = {}

        def fps(b):
            if b < 4:
                return ff_ps4[:, b, :]
            return ffstate["ext"][:, b - 4, :]

        def fpair(b0):
            if b0 < 4:
                return ff_ps4.ap()[:, b0:b0 + 2, :].rearrange("p a b -> p (a b)")
            return ffstate["ext"].ap()[:, b0 - 4:b0 - 2, :].rearrange("p a b -> p (a b)")

        def nbanks():
            return 8 if ffstate["ext"] is not None else 4

        wblocks = []
        for u in range(17):
            gu, dn = ((wsh_gu_in, wsh_d_in) if u == 0
                      else (wexp_gu_in[u - 1], wexp_d_in[u - 1]))
            for b8 in range(8):
                wblocks.append((gu[b8].rearrange("p a b -> p (a b)"), u, "gu", b8))
            for j4 in range(4):
                wblocks.append((dn[j4].rearrange("p a b c -> p (a b c)"), u, "dn", j4))
        NWB = len(wblocks)
        widx = {(u, k, i): w for w, (_, u, k, i) in enumerate(wblocks)}

        wslot_map = {}
        wsw = [None, -1]

        wcand = set()

        def wdma(w):
            if w < NWB:
                wcand.add(w)
            for ww in sorted(wcand):
                if ("wdma", ww) in E:
                    wcand.discard(ww)
                    continue
                if wsw[0] is None or ww < wsw[0]:
                    slot = ww % 2
                    prevw = ww - 2
                else:
                    slot = (ww - wsw[0]) % 3
                    prevw = ww - 3 if (ww - wsw[0]) >= 3 else wsw[0] - 1
                if prevw >= 0 and ("ffuse", prevw) not in E:
                    break
                src = wblocks[ww][0]
                wslot_map[ww] = slot
                wsw[1] = max(wsw[1], ww)
                E[("wdma", ww)] = dma(
                    "G", lambda g, src=src, slot=slot: g.dma_start(out=wring[slot][:], in_=src),
                    [(tc, E.get(("ffuse", prevw)))])
                wcand.discard(ww)

        def ff_gen(u):
            ue = u - 1
            wdma(widx[(u, "gu", 0)])
            wdma(widx[(u, "gu", 0)] + 1)
            wdma(widx[(u, "gu", 0)] + 2)
            if u >= 1:
                E[("cbc", u)] = dma(
                    "S", lambda s, ue=ue: s.dma_start(
                        out=comb_bc[:], in_=d_comb[ue:ue + 1, :].partition_broadcast(128)),
                    [(vc, E.get(("combuse", u - 1))), (vc, E["combrdy"])])
            yield

            for i in range(4):
                pg = u * 4 + i
                for gu01 in (0, 1):
                    w = widx[(u, "gu", 2 * i + gu01)]
                    wdma(w)
                    wv = wring[wslot_map[w]].ap().rearrange("p (a b) -> p a b", a=NC)
                    nb = nbanks()
                    b0g = (ffgrp[0] % (nb // 4)) * 4
                    ffgrp[0] += 1
                    bw = [ffbank[b0g + b] for b in range(4)]

                    def mk_gu(t, wv=wv, b0g=b0g):
                        for jg01 in (0, 1):
                            for c in range(NC):
                                st = (c == 0)
                                sp = (c == NC - 1)
                                t.matmul(out=fps(b0g + jg01 * 2 + 0),
                                         lhsT=wv[:, c, jg01 * 128:(jg01 + 1) * 128],
                                         rhs=xn_own[:, c, 0:512], start=st, stop=sp)
                                bi = t.matmul(out=fps(b0g + jg01 * 2 + 1),
                                              lhsT=wv[:, c, jg01 * 128:(jg01 + 1) * 128],
                                              rhs=xn_own[:, c, 512:1024], start=st, stop=sp)
                        return bi
                    mm = kb.emit("T", mk_gu,
                                 waits=[E[("wdma", w)], (vc, E["xnrdy"])] + bw, inc=tc)
                    E[("ffuse", w)] = mm
                    E[("gumm", u, i, gu01)] = mm
                    wdma(w + 2)
                    wdma(w + 3)

                    if gu01 == 0:
                        for jg01 in (0, 1):
                            for th in (0, 1):
                                ev = kb.emit(
                                    "S", lambda s, jg01=jg01, th=th, b0g=b0g: s.activation(
                                        out=silu_t[:, jg01, th * 512:(th + 1) * 512],
                                        in_=fps(b0g + jg01 * 2 + th), func=AF.Silu),
                                    waits=[(tc, mm), (vc, E.get(("upev_b", pg - 1, 1)))],
                                    inc=sc)
                                E[("silu", u, i, jg01, th)] = ev
                                ffbank[b0g + jg01 * 2 + th] = (sc, ev)
                    else:
                        if u >= 1:
                            scev = kb.emit(
                                "V", lambda v: v.tensor_mul(
                                    out=silu_t[:], in0=silu_t[:],
                                    in1=comb_bc.ap().unsqueeze(1).to_broadcast([128, 2, TOK])),
                                waits=[(sc, E[("silu", u, i, 1, 1)]), E[("cbc", u)]],
                                inc=vc)
                            E[("combuse", u)] = scev
                        for jg01 in (0, 1):
                            ev = kb.emit(
                                "V", lambda v, i=i, jg01=jg01, b0g=b0g: v.tensor_mul(
                                    out=h_sc[:, 2 * i + jg01, :],
                                    in0=fpair(b0g + jg01 * 2),
                                    in1=silu_t[:, jg01, :]),
                                waits=[(tc, mm),
                                       ((vc, scev) if u >= 1
                                        else (sc, E[("silu", u, i, jg01, 1)])),
                                       (tc, E.get(("dnmmlast", u - 1)))], inc=vc)
                            E[("upev_b", pg, jg01)] = ev
                            ffbank[b0g + jg01 * 2 + 0] = (vc, ev)
                            ffbank[b0g + jg01 * 2 + 1] = (vc, ev)
                yield

            upw = [(vc, E[("upev_b", u * 4 + 3, jj)]) for jj in (0, 1)]

            for j4 in range(4):
                w = widx[(u, "dn", j4)]
                wdma(w)
                wv = wring[wslot_map[w]].ap().rearrange("p (a b) -> p a b", a=8)
                for m4 in range(4):
                    dchunk = 4 * j4 + m4
                    nb = nbanks()
                    bank0 = (ffgrp[0] % (nb // 2)) * 2
                    ffgrp[0] += 1
                    bw = [ffbank[bank0], ffbank[bank0 + 1]]

                    def mk_dn(t, wv=wv, m4=m4, bank0=bank0):
                        for fc in range(8):
                            st = (fc == 0)
                            sp = (fc == 7)
                            t.matmul(out=fps(bank0 + 0),
                                     lhsT=wv[:, fc, m4 * 128:(m4 + 1) * 128],
                                     rhs=h_sc[:, fc, 0:512], start=st, stop=sp)
                            bi = t.matmul(out=fps(bank0 + 1),
                                          lhsT=wv[:, fc, m4 * 128:(m4 + 1) * 128],
                                          rhs=h_sc[:, fc, 512:1024], start=st, stop=sp)
                        return bi
                    mm = kb.emit("T", mk_dn, waits=[E[("wdma", w)]] + bw + upw, inc=tc)
                    E[("ffuse", w)] = mm
                    E[("dnmm", u, j4, m4)] = mm
                    if m4 == 3:
                        wdma(w + 2)
                        wdma(w + 3)
                    def mk_dnev(v, u=u, dchunk=dchunk, bank0=bank0):
                        od = (slice(None), dchunk, slice(None))
                        if u == 0:
                            return v.tensor_copy(out=out_acc[od], in_=fpair(bank0))
                        return v.tensor_add(out=out_acc[od], in0=fpair(bank0),
                                            in1=out_acc[od])
                    ev = kb.emit("V", mk_dnev,
                                 waits=[(tc, mm), (vc, oacc.get(dchunk))], inc=vc)
                    oacc[dchunk] = ev
                    ffbank[bank0 + 0] = (vc, ev)
                    ffbank[bank0 + 1] = (vc, ev)
                    if m4 % 2 == 1:
                        yield
            E[("dnmmlast", u)] = E[("dnmm", u, 3, 3)]
            yield

        ff_units = [ff_gen(u) for u in range(17)]

        class FFPull:
            def __init__(self, gens):
                self.gens = list(gens)
                self.cur = 0
                self.budget = 0.0

            def pull(self, k):
                self.budget += k
                while self.budget >= 1.0 and self.cur < len(self.gens):
                    try:
                        next(self.gens[self.cur])
                    except StopIteration:
                        self.cur += 1
                        continue
                    self.budget -= 1.0

            def drain(self):
                while self.cur < len(self.gens):
                    try:
                        next(self.gens[self.cur])
                    except StopIteration:
                        self.cur += 1

        ffp = FFPull(ff_units)
        NQUANTA = 17.0 * 14.0

        with ExitStack() as pre:
            ep = pre.enter_context
            alog_sb = ep(nc.sbuf_tensor("alog_sb", [128, 2, NS], F32))
            onec = ep(nc.sbuf_tensor("onec", [128, 2], F32))
            bdt_sb = ep(nc.sbuf_tensor("bdt_sb", [128, 2], F32))
            dpar_sb = ep(nc.sbuf_tensor("dpar_sb", [128, 2], F32))
            ebias_sb = ep(nc.sbuf_tensor("ebias_sb", [128, NE], F32))
            gw_f = ep(nc.sbuf_tensor("gw_f", [128, NC, NE], F32))
            wbc_bf = ep(nc.sbuf_tensor("wbc_bf", [128, NC, 128], BF16))
            wdt_bf = ep(nc.sbuf_tensor("wdt_bf", [RK, 2, 128], BF16))
            ident_f = ep(nc.sbuf_tensor("ident_f", [128, 128], F32))
            ones_bcol = ep(nc.sbuf_tensor("ones_bcol", [128, 1], BF16))
            ones_f = ep(nc.sbuf_tensor("ones_f", [1, 128], F32))
            eps_sb = ep(nc.sbuf_tensor("eps_sb", [1, 1], F32))
            x_f = [ep(nc.sbuf_tensor(f"x_f{i}", [128, NC, PC], F32)) for i in range(2)]
            x_bf = ep(nc.sbuf_tensor("x_bf0", [128, NC, PC], BF16))
            xn_tmp = x_bf
            mean_sb = ep(nc.sbuf_tensor("mean_sb", [1, 2, PC], F32))
            rstd_sb = ep(nc.sbuf_tensor("rstd_sb", [1, PC], F32))
            dtr_bf = [ep(nc.sbuf_tensor("dtr_bf0", [RK, PC], BF16))] * 2
            st_B = [ep(nc.sbuf_tensor("st_B0", [NS, PC], BF16))] * 2
            st_C = [ep(nc.sbuf_tensor("st_C0", [NS, PC], BF16))] * 2
            sp_t = ep(nc.sbuf_tensor("sp_t", [128, 2, PC], F32))
            x_sl = sp_t
            dneg = [ep(nc.sbuf_tensor("dneg0", [128, 2, PC], F32))] * 2
            duneg = [ep(nc.sbuf_tensor("duneg0", [128, 2, PC], F32))] * 2
            uDc = ep(nc.sbuf_tensor("uDc0", [128, 2, PC], F32))
            u_c = [ep(nc.sbuf_tensor("u_c0", [128, 2, PC], F32))] * 2
            c_sc = ep(nc.sbuf_tensor("c_sc", [128, 2, NE], F32))
            c_sel = ep(nc.sbuf_tensor("c_sel", [128, 2, NE], F32))
            c_eq1 = ep(nc.sbuf_tensor("c_eq1", [128, 2, NE], F32))
            c_eq2 = ep(nc.sbuf_tensor("c_eq2", [128, 2, NE], F32))
            c_m = ep(nc.sbuf_tensor("c_m", [128, 2], F32))
            c_den = ep(nc.sbuf_tensor("c_den", [128, 2], F32))
            c_out = ep(nc.sbuf_tensor("c_out", [128, 2, NE], F32))
            pb_stat = ep(nc.psum_tensor("pb_stat", [1, 2, PC], F32))
            pb_ln = ep(nc.psum_tensor("pb_ln", [128, 2, PC], F32))
            pb_dbc = ep(nc.psum_tensor("pb_dbc", [128, PC], F32))
            pb_misc = ep(nc.psum_tensor("pb_misc", [128, 512], F32))

            E["ld_wbc"] = dma("G", lambda g: g.dma_start(out=wbc_bf[:], in_=wbc_in))
            E["ld_wdt"] = dma("G", lambda g: g.dma_start(out=wdt_bf[:], in_=wdt_in))
            E["ld_gwf"] = dma("Q", lambda q: q.dma_start(out=gw_f[:], in_=gw_in))
            E["ld_id"] = dma("G", lambda g: g.dma_start(out=ident_f[:], in_=ident_in))
            E["ld_bdt"] = dma("Q", lambda q: q.dma_start(out=bdt_sb[:], in_=bdt_in))
            E["ld_alog"] = dma("Q", lambda q: q.dma_start(out=alog_sb[:], in_=alog_in))
            E["ld_dpar"] = dma("Q", lambda q: q.dma_start(out=dpar_sb[:], in_=dpar_in))
            E["ld_ebias"] = dma("Q", lambda q: q.dma_start(out=ebias_sb[:], in_=ebias_in))

            kb.emit("V", lambda v: v.memset(ones_bcol[:], float(np.float32(1.0 / D))))
            kb.emit("V", lambda v: v.memset(ones_f[:], 1.0))
            E["veps"] = kb.emit("V", lambda v: v.memset(eps_sb[:], 1e-5), inc=vc)
            E["onec"] = kb.emit("V", lambda v: v.memset(onec[:], 1.0), inc=vc)
            E["hzero"] = kb.emit("V", lambda v: v.memset(h_state[:], 0.0), inc=vc)
            E["apos0"] = kb.emit("S", lambda s: s.activation(out=A_pos[:], in_=alog_sb[:], func=AF.Exp),
                                 waits=[E["ld_alog"]], inc=sc)
            E["apos"] = kb.emit("V", lambda v: v.tensor_scalar_mul(out=A_pos[:], in0=A_pos[:], scalar1=-1.0),
                                waits=[(sc, E["apos0"])], inc=vc)

            ITSEQ = list(range(NPC, NPC + 4)) + list(range(NPC))

            def stageA(pos):
                it = ITSEQ[pos]
                itp1 = ITSEQ[pos - 1] if pos >= 1 else None
                itp2 = ITSEQ[pos - 2] if pos >= 2 else None
                own = it >= NPC
                r = pos % 2
                if own:
                    j = it - NPC

                    def src_ap(e_, j=j):
                        return x_own_in[j]
                else:
                    def src_ap(e_, it=it):
                        return x_in[it]

                E[("xf", it)] = dma(
                    "Q", lambda q, sa=src_ap, r=r: q.dma_start(out=x_f[r][:], in_=sa(q)),
                    [(vc, E.get(("xfdone", itp2))), (tc, E.get(("rt", itp2)))])
                E[("xbf", it)] = dma(
                    "G", lambda g, sa=src_ap: g.dma_start(out=x_bf[:], in_=sa(g)),
                    [(tc, E.get(("stats", itp1))), (tc, E.get(("dbc", itp1)))])
                def mk_statsA(t):
                    for c in range(NC):
                        bi = t.matmul(out=pb_stat[:, 0, :], lhsT=ones_bcol[:], rhs=x_bf[:, c, :],
                                      start=(c == 0), stop=(c == NC - 1))
                    return bi
                E[("statsA", it)] = kb.emit(
                    "T", mk_statsA,
                    waits=[E[("xbf", it)], (vc, E["hzero"]),
                           (vc, E.get(("rows0", itp1)))], inc=tc)
                E[("xsq", it)] = kb.emit(
                    "S", lambda s, r=r: s.activation(out=x_bf[:], in_=x_f[r][:], func=AF.Square),
                    waits=[E[("xf", it)], (tc, E[("statsA", it)])], inc=sc)

                def mk_statsB(t):
                    for c in range(NC):
                        bi = t.matmul(out=pb_stat[:, 1, :], lhsT=ones_bcol[:], rhs=x_bf[:, c, :],
                                      start=(c == 0), stop=(c == NC - 1))
                    return bi
                E[("stats", it)] = kb.emit(
                    "T", mk_statsB, waits=[(sc, E[("xsq", it)])], inc=tc)

                def mk_rows0(v):
                    v.tensor_copy(out=mean_sb[:, 0, :], in_=pb_stat[:, 0, :])
                    return v.tensor_copy(out=mean_sb[:, 1, :], in_=pb_stat[:, 1, :])
                E[("rows0", it)] = kb.emit(
                    "V", mk_rows0,
                    waits=[(tc, E[("stats", it)]), (vc, E.get(("rows3", itp1))),
                           (tc, E.get(("lnbc", itp1)))], inc=vc)
                E[("rows1", it)] = kb.emit(
                    "V", lambda v: v.tensor_mul(out=rstd_sb[:], in0=mean_sb[:, 0, :], in1=mean_sb[:, 0, :]),
                    waits=[(vc, E[("rows0", it)])], inc=vc)
                E[("rows2", it)] = kb.emit(
                    "V", lambda v: v.tensor_sub(out=mean_sb[:, 1, :], in0=mean_sb[:, 1, :], in1=rstd_sb[:]),
                    waits=[(vc, E[("rows1", it)])], inc=vc)
                E[("rows3a", it)] = kb.emit(
                    "S", lambda s: s.activation(out=mean_sb[:, 1, :], in_=mean_sb[:, 1, :],
                                                func=AF.Sqrt, bias=eps_sb[:]),
                    waits=[(vc, E[("rows2", it)]), (vc, E["veps"])], inc=sc)
                E[("rows3", it)] = kb.emit(
                    "V", lambda v: v.reciprocal(out=mean_sb[:, 1, :], in_=mean_sb[:, 1, :]),
                    waits=[(sc, E[("rows3a", it)])], inc=vc)

                def mk_bcast(t):
                    t.matmul(out=pb_ln[:, 0, :], lhsT=ones_f[:], rhs=mean_sb[0:1, 0, :],
                             start=True, stop=True)
                    return t.matmul(out=pb_ln[:, 1, :], lhsT=ones_f[:], rhs=mean_sb[0:1, 1, :],
                                    start=True, stop=True)
                E[("lnbc", it)] = kb.emit(
                    "T", mk_bcast,
                    waits=[(vc, E[("rows3", it)]), (vc, E.get(("ub", itp1))),
                           (vc, E.get(("xfdone", itp1))), (vc, E["hzero"])], inc=tc)

            def stageB1(pos):
                it = ITSEQ[pos]
                itp1 = ITSEQ[pos - 1] if pos >= 1 else None
                itp2 = ITSEQ[pos - 2] if pos >= 2 else None
                own = it >= NPC
                r = pos % 2
                j = it - NPC

                E[("xc", it)] = kb.emit(
                    "V", lambda v, r=r: v.tensor_sub(
                        out=x_f[r][:], in0=x_f[r][:],
                        in1=pb_ln.ap()[:, 0:1, :].to_broadcast([128, NC, PC])),
                    waits=[(tc, E[("lnbc", it)]), (sc, E[("xsq", it)])], inc=vc)

                if own:
                    E[("xn2", it)] = kb.emit(
                        "V", lambda v, r=r: v.tensor_mul(
                            out=x_f[r][:], in0=x_f[r][:],
                            in1=pb_ln.ap()[:, 1:2, :].to_broadcast([128, NC, PC])),
                        waits=[(vc, E[("xc", it)])], inc=vc)
                    E[("xncast", it)] = kb.emit(
                        "V", lambda v, j=j, r=r: v.tensor_copy(
                            out=xn_own[:, :, j * PC:(j + 1) * PC], in_=x_f[r][:]),
                        waits=[(vc, E[("xn2", it)])], inc=vc)
                    E[("xfdone", it)] = E[("xncast", it)]
                    if j == 3:
                        E["xnrdy"] = E[("xncast", it)]
                else:
                    E[("xn", it)] = kb.emit(
                        "V", lambda v, r=r: v.tensor_mul(
                            out=xn_tmp[:], in0=x_f[r][:],
                            in1=pb_ln.ap()[:, 1:2, :].to_broadcast([128, NC, PC])),
                        waits=[(vc, E[("xc", it)]), (tc, E[("stats", it)]),
                               (tc, E.get(("dbc", itp1)))], inc=vc)
                    E[("xfdone", it)] = E[("xn", it)]

                if not own:
                    E[("xsl", it)] = dma(
                        "Q", lambda q, it=it: q.dma_start(out=x_sl[:], in_=x_sl_in[it]),
                        [(vc, E.get(("ub", itp1))), (vc, E.get(("dsum", itp1)))])
                    E[("ua", it)] = kb.emit(
                        "V", lambda v: v.tensor_sub(
                            out=x_sl[:], in0=x_sl[:],
                            in1=pb_ln.ap()[:, 0:1, :].to_broadcast([128, 2, PC])),
                        waits=[E[("xsl", it)], (tc, E[("lnbc", it)])], inc=vc)
                    E[("ub", it)] = kb.emit(
                        "V", lambda v, r=r: v.tensor_mul(
                            out=u_c[r][:], in0=x_sl[:],
                            in1=pb_ln.ap()[:, 1:2, :].to_broadcast([128, 2, PC])),
                        waits=[(vc, E[("ua", it)]), (gc, E.get(("uD", itp1))),
                               (gc, E.get(("du", itp1)))], inc=vc)

                    def mk_dbc(t):
                        for c in range(NC):
                            bi = t.matmul(out=pb_dbc[:], lhsT=wbc_bf[:, c, :], rhs=xn_tmp[:, c, :],
                                          start=(c == 0), stop=(c == NC - 1))
                        return bi
                    E[("dbc", it)] = kb.emit(
                        "T", mk_dbc,
                        waits=[(vc, E[("xn", it)]), E["ld_wbc"],
                               (vc, E.get(("dbcev", itp1)))], inc=tc)

                    def mk_dbcev(v, r=r):
                        v.tensor_copy(out=dtr_bf[r][:], in_=pb_dbc.ap()[0:RK, :])
                        v.tensor_copy(out=st_B[r][:], in_=pb_dbc.ap()[RK:RK + NS, :])
                        return v.tensor_copy(out=st_C[r][:], in_=pb_dbc.ap()[RK + NS:RK + 2 * NS, :])
                    E[("dbcev", it)] = kb.emit(
                        "V", mk_dbcev,
                        waits=[(tc, E[("dbc", it)]), (tc, E.get(("dt", itp1))),
                               E.get(("wBd", itp1)), E.get(("wCd", itp1))], inc=vc)

            def stageB2(pos):
                it = ITSEQ[pos]
                itp1 = ITSEQ[pos - 1] if pos >= 1 else None
                itp2 = ITSEQ[pos - 2] if pos >= 2 else None
                own = it >= NPC
                r = pos % 2
                j = it - NPC

                if not own:
                    E[("wBd", it)] = dma(
                        "Q", lambda q, it=it, r=r: q.dma_start(
                            out=d_B[:, it * PC:(it + 1) * PC], in_=st_B[r][:]),
                        [(vc, E[("dbcev", it)])])
                    E[("wCd", it)] = dma(
                        "Q", lambda q, it=it, r=r: q.dma_start(
                            out=d_C[:, it * PC:(it + 1) * PC], in_=st_C[r][:]),
                        [(vc, E[("dbcev", it)])])

                    def mk_dt(t, r=r):
                        t.matmul(out=pb_misc[:, 0:PC], lhsT=wdt_bf[:, 0, :], rhs=dtr_bf[r][:],
                                 start=True, stop=True)
                        return t.matmul(out=pb_misc[:, PC:2 * PC], lhsT=wdt_bf[:, 1, :], rhs=dtr_bf[r][:],
                                        start=True, stop=True)
                    E[("dt", it)] = kb.emit(
                        "T", mk_dt,
                        waits=[(vc, E[("dbcev", it)]), E["ld_wdt"], (vc, E.get(("cfm", itp1))),
                               (sc, E.get(("sp4", itp1)))], inc=tc)

                    pmv = pb_misc.ap()[:, 0:2 * PC].rearrange("p (a b) -> p a b", a=2)

                    def mk_sp1(s, pmv=pmv):
                        s.activation(out=sp_t[:, 0, :], in_=pmv[:, 0, :], func=AF.Abs,
                                     bias=bdt_sb[:, 0:1])
                        return s.activation(out=sp_t[:, 1, :], in_=pmv[:, 1, :], func=AF.Abs,
                                            bias=bdt_sb[:, 1:2])
                    E[("sp1", it)] = kb.emit(
                        "S", mk_sp1,
                        waits=[(tc, E[("dt", it)]), E["ld_bdt"], (vc, E[("ub", it)]),
                               (sc, E.get(("sp3", itp1))), (vc, E.get(("dsum", itp1)))], inc=sc)
                    E[("sp2", it)] = kb.emit(
                        "S", lambda s: s.activation(out=sp_t[:], in_=sp_t[:], func=AF.Exp, scale=-1.0),
                        waits=[(sc, E[("sp1", it)])], inc=sc)
                    E[("sp3", it)] = kb.emit(
                        "S", lambda s: s.activation(out=sp_t[:], in_=sp_t[:], func=AF.Ln, bias=onec[:, 0:1]),
                        waits=[(sc, E[("sp2", it)]), (vc, E["onec"])], inc=sc)

                    def mk_sp4(s, r=r, pmv=pmv):
                        s.activation(out=dneg[r][:, 0, :], in_=pmv[:, 0, :], func=AF.Relu,
                                     bias=bdt_sb[:, 0:1])
                        return s.activation(out=dneg[r][:, 1, :], in_=pmv[:, 1, :], func=AF.Relu,
                                            bias=bdt_sb[:, 1:2])
                    E[("sp4", it)] = kb.emit(
                        "S", mk_sp4,
                        waits=[(sc, E[("sp3", it)]), E.get(("wdn", itp1)),
                               (gc, E.get(("du", itp1)))], inc=sc)
                    E[("dsum", it)] = kb.emit(
                        "V", lambda v, r=r: v.tensor_add(out=dneg[r][:], in0=dneg[r][:], in1=sp_t[:]),
                        waits=[(sc, E[("sp4", it)])], inc=vc)

                    E[("du", it)] = kb.emit(
                        "G", lambda g, r=r: g.tensor_mul(out=duneg[r][:], in0=dneg[r][:], in1=u_c[r][:]),
                        waits=[(vc, E[("dsum", it)]), (vc, E[("ub", it)]),
                               E.get(("wdu", itp1))], inc=gc)

                    def mk_uD(g, r=r):
                        g.tensor_scalar_mul(out=uDc[:, 0, :], in0=u_c[r][:, 0, :], scalar1=dpar_sb[:, 0:1])
                        return g.tensor_scalar_mul(out=uDc[:, 1, :], in0=u_c[r][:, 1, :],
                                                   scalar1=dpar_sb[:, 1:2])
                    E[("uD", it)] = kb.emit(
                        "G", mk_uD,
                        waits=[(vc, E[("ub", it)]), E["ld_dpar"],
                               E.get(("wuD", itp1))], inc=gc)

                    tsl = slice(it * PC, (it + 1) * PC)
                    E[("wdn", it)] = dma(
                        "S", lambda s, r=r, tsl=tsl: s.dma_start(
                            out=d_delta.rearrange("cl p t -> p cl t")[:, :, tsl], in_=dneg[r][:]),
                        [(vc, E[("dsum", it)])])
                    E[("wdu", it)] = dma(
                        "G", lambda g, r=r, tsl=tsl: g.dma_start(
                            out=d_du.rearrange("cl p t -> p cl t")[:, :, tsl], in_=duneg[r][:]),
                        [(gc, E[("du", it)])])
                    E[("wuD", it)] = dma(
                        "S", lambda s, tsl=tsl: s.dma_start(
                            out=d_uD.rearrange("cl p t -> p cl t")[:, :, tsl], in_=uDc[:]),
                        [(gc, E[("uD", it)])])
                else:
                    def mk_rt(t, r=r):
                        for tt in range(2):
                            for c in range(NC):
                                bi = t.matmul(
                                    out=pb_misc[:, tt * NE:(tt + 1) * NE],
                                    lhsT=x_f[r][:, c, tt * 128:(tt + 1) * 128],
                                    rhs=gw_f[:, c, :],
                                    start=(c == 0), stop=(c == NC - 1))
                        return bi
                    E[("rt", it)] = kb.emit(
                        "T", mk_rt,
                        waits=[(vc, E[("xn2", it)]), (sc, E.get(("csc", itp1))),
                               (vc, E.get(("cfm", itp1))), E["ld_gwf"]], inc=tc)

                    E[("csc", it)] = kb.emit(
                        "S", lambda s: s.activation(
                            out=c_sc[:],
                            in_=pb_misc.ap()[:, 0:2 * NE].rearrange("p (a b) -> p a b", a=2),
                            func=AF.Sigmoid),
                        waits=[(tc, E[("rt", it)]), (vc, E.get(("cw", itp1)))], inc=sc)

                    def vch(mk, extra=()):
                        return kb.emit("V", mk, waits=[(vc, vc.v)] + list(extra), inc=vc)

                    vch(lambda v: v.tensor_add(
                        out=c_sel[:], in0=c_sc[:],
                        in1=ebias_sb.ap().unsqueeze(1).to_broadcast([128, 2, NE])),
                        extra=[(sc, E[("csc", it)]), (vc, E.get(("cfm", itp1))),
                               (tc, E.get(("ctr", itp1))), E["ld_ebias"]])
                    vch(lambda v: v.tensor_reduce(out=c_m[:], in_=c_sel[:], axis=AX.X, op=ALU.max))

                    def mk_eq1(v):
                        v.tensor_scalar(out=c_eq1[:, 0, :], in0=c_sel[:, 0, :], scalar1=c_m[:, 0:1],
                                        scalar2=None, op0=ALU.is_ge)
                        return v.tensor_scalar(out=c_eq1[:, 1, :], in0=c_sel[:, 1, :],
                                               scalar1=c_m[:, 1:2], scalar2=None, op0=ALU.is_ge)
                    vch(mk_eq1)
                    vch(lambda v: v.scalar_tensor_tensor(out=c_sel[:], in0=c_eq1[:], scalar=-1e9,
                                                         in1=c_sel[:], op0=ALU.mult, op1=ALU.add))
                    vch(lambda v: v.tensor_reduce(out=c_m[:], in_=c_sel[:], axis=AX.X, op=ALU.max))

                    def mk_eq2(v):
                        v.tensor_scalar(out=c_eq2[:, 0, :], in0=c_sel[:, 0, :], scalar1=c_m[:, 0:1],
                                        scalar2=None, op0=ALU.is_ge)
                        return v.tensor_scalar(out=c_eq2[:, 1, :], in0=c_sel[:, 1, :],
                                               scalar1=c_m[:, 1:2], scalar2=None, op0=ALU.is_ge)
                    vch(mk_eq2)
                    vch(lambda v: v.tensor_add(out=c_eq1[:], in0=c_eq1[:], in1=c_eq2[:]))
                    vch(lambda v: v.tensor_mul(out=c_eq2[:], in0=c_sc[:], in1=c_eq1[:]))
                    vch(lambda v: v.tensor_reduce(out=c_den[:], in_=c_eq2[:], axis=AX.X, op=ALU.add))
                    vch(lambda v: v.tensor_scalar_add(out=c_den[:], in0=c_den[:], scalar1=1e-20))
                    vch(lambda v: v.reciprocal(out=c_den[:], in_=c_den[:]))

                    def mk_cw(v):
                        v.tensor_scalar(out=c_out[:, 0, :], in0=c_eq2[:, 0, :], scalar1=c_den[:, 0:1],
                                        scalar2=None, op0=ALU.mult)
                        return v.tensor_scalar(out=c_out[:, 1, :], in0=c_eq2[:, 1, :],
                                               scalar1=c_den[:, 1:2], scalar2=None, op0=ALU.mult)
                    E[("cw", it)] = vch(mk_cw)

                    def mk_ctr(t):
                        t.transpose(out=pb_misc.ap()[0:NE, 64:192], in_=c_out[:, 0, :],
                                    identity=ident_f[:])
                        return t.transpose(out=pb_misc.ap()[0:NE, 192:320], in_=c_out[:, 1, :],
                                           identity=ident_f[:])
                    E[("ctr", it)] = kb.emit(
                        "T", mk_ctr,
                        waits=[(vc, E[("cw", it)]), E["ld_id"], (sc, E[("csc", it)]),
                               (vc, E.get(("cfm", itp1)))], inc=tc)

                    def mk_cfm(v, j=j):
                        v.tensor_copy(out=comb_fm[:, j * PC: j * PC + 128],
                                      in_=pb_misc.ap()[0:NE, 64:192])
                        return v.tensor_copy(out=comb_fm[:, j * PC + 128: (j + 1) * PC],
                                             in_=pb_misc.ap()[0:NE, 192:320])
                    E[("cfm", it)] = kb.emit("V", mk_cfm, waits=[(tc, E[("ctr", it)])], inc=vc)
                    E[("wcomb", it)] = dma(
                        "Q", lambda q, j=j: q.dma_start(
                            out=d_comb[:, j * PC:(j + 1) * PC],
                            in_=comb_fm[:, j * PC:(j + 1) * PC]),
                        [(vc, E[("cfm", it)])])
                    if j == 3:
                        E["combrdy"] = E[("cfm", it)]

            NIT = NPC + 4
            for pos in range(NIT + 1):
                if pos >= 1:
                    stageB1(pos - 1)
                if pos < NIT:
                    stageA(pos)
                if pos >= 1:
                    stageB2(pos - 1)
                if pos >= 6:
                    ffp.pull(2.0)

            BAR = [(tc, tc.v), (vc, vc.v), (sc, sc.v), (gc, gc.v)] + allq()
            kb.replay(block)

        if "xn" in dbg:
            dma("Q", lambda q: q.dma_start(out=dbg_aps["dbg_xn"], in_=xn_own[:]), BAR)
        if "comb" in dbg:
            dma("Q", lambda q: q.dma_start(out=dbg_aps["dbg_comb"], in_=comb_fm[:]), BAR)
        if "pre" in dbg:
            dma("Q", lambda q: q.dma_start(out=dbg_aps["dbg_delta"], in_=d_delta), BAR)
            dma("Q", lambda q: q.dma_start(out=dbg_aps["dbg_du"], in_=d_du), BAR)
            dma("Q", lambda q: q.dma_start(out=dbg_aps["dbg_uD"], in_=d_uD), BAR)
            dma("Q", lambda q: q.dma_start(out=dbg_aps["dbg_B"], in_=d_B), BAR)
            dma("Q", lambda q: q.dma_start(out=dbg_aps["dbg_C"], in_=d_C), BAR)

        if upto == "pre":
            ffp.drain()
            fv = dma("Q", lambda q: q.dma_start(out=out_ap, in_=out_acc[:]),
                     BAR + [(tc, tc.v), (vc, vc.v), (sc, sc.v)])
            kb.emit("Q", lambda q: None, waits=[fv])
            kb.replay(block)
            return nc

        with ExitStack() as main:
            em = main.enter_context
            SRING = 3
            dblk = em(nc.sbuf_tensor("dblk", [128, 2, TB], F32))
            dublk = em(nc.sbuf_tensor("dublk", [128, 2, TB], BF16))
            yacc = em(nc.sbuf_tensor("yacc", [128, 2, TB], F32))
            dA = [em(nc.sbuf_tensor(f"dA{i}", [128, TB], F32)) for i in range(SRING)]
            dBu = [em(nc.sbuf_tensor(f"dBu{i}", [128, TB], F32)) for i in range(SRING)]
            hser = em(nc.sbuf_tensor("hser", [128, 2, TB], BF16))
            Bbc = [em(nc.sbuf_tensor(f"Bbc{i}", [128, TB], BF16)) for i in range(2)]
            Cbc = [em(nc.sbuf_tensor(f"Cbc{i}", [128, TB], BF16)) for i in range(2)]
            ffstate["ext"] = em(nc.psum_tensor("ff_ps_ext", [128, 4, 512], F32))
            wring.append(em(nc.sbuf_tensor("wring2", [128, 4096], BF16)))
            wsw[0] = wsw[1] + 1
            for eng in ENGS:
                kb.emit(eng, lambda e_: None, waits=BAR)

            def emit_gymul(b, bn):
                n = bn % NS
                npp = bn % 2
                E[("hst", bn)] = kb.emit(
                    "G", lambda g, n=n: g.tensor_copy(
                        out=h_state[:, :, n:n + 1], in_=hser[:, :, TB - 1:TB]),
                    waits=[(vc, E[("scan", bn * 2 + 1)])], inc=gc)
                E[("ymul_g", bn)] = kb.emit(
                    "G", lambda g, npp=npp: g.tensor_mul(
                        out=hser[:], in0=hser[:],
                        in1=Cbc[npp].ap().unsqueeze(1).to_broadcast([128, 2, TB])),
                    waits=[(gc, E[("hst", bn)]), E[("bcC", bn)]], inc=gc)

            def emit_yadds(b, bn):
                first = (bn % NS) == 0

                def mk_y(v, first=first):
                    if first:
                        return v.tensor_copy(out=yacc[:], in_=hser[:])
                    return v.tensor_add(out=yacc[:], in0=yacc[:], in1=hser[:])
                E[("yadd", bn)] = kb.emit(
                    "V", mk_y,
                    waits=[(gc, E[("ymul_g", bn)]), (vc, E.get(("yadd", bn - 1))),
                           E.get(("wag", b - 1)) if first else None], inc=vc)

            def sc_block(b):
                t0 = b * TB
                last_it = 4 * b + 3
                E[("dblk", b)] = dma(
                    "Q", lambda q, t0=t0: q.dma_start(
                        out=dblk[:], in_=d_delta.rearrange("cl p t -> p cl t")[:, :, t0:t0 + TB]),
                    [E[("wdn", last_it)], (sc, E.get(("dAlast", b - 1)))])
                E[("dublk", b)] = dma(
                    "Q", lambda q, t0=t0: q.dma_start(
                        out=dublk[:], in_=d_du.rearrange("cl p t -> p cl t")[:, :, t0:t0 + TB]),
                    [E[("wdu", last_it)], (gc, E.get(("dBulast", b - 1)))])

                for n in range(NS):
                    bn = b * NS + n
                    npp = bn % 2

                    E[("bcB", bn)] = dma(
                        "S", lambda s, n=n, t0=t0, npp=npp: s.dma_start(
                            out=Bbc[npp][:], in_=d_B[n:n + 1, t0:t0 + TB].partition_broadcast(128)),
                        [(gc, E.get(("dBu_g", bn - 2, 1))), E[("wBd", last_it)]])
                    E[("bcC", bn)] = dma(
                        "Q", lambda q, n=n, t0=t0, npp=npp: q.dma_start(
                            out=Cbc[npp][:], in_=d_C[n:n + 1, t0:t0 + TB].partition_broadcast(128)),
                        [(gc, E.get(("ymul_g", bn - 2))), E[("wCd", last_it)]])

                    ffp.pull((NQUANTA - 36.0) / 256.0)

                    if n >= 1:
                        emit_gymul(b, bn - 1)
                        emit_yadds(b, bn - 1)

                    for c in (0, 1):
                        gi = bn * 2 + c
                        gp = gi % SRING
                        E[("dA", gi)] = kb.emit(
                            "S", lambda s, c=c, n=n, gp=gp: s.activation(
                                out=dA[gp][:], in_=dblk[:, c, :], func=AF.Exp,
                                scale=A_pos[:, c, n:n + 1]),
                            waits=[E[("dblk", b)], (vc, E["apos"]),
                                   (vc, E.get(("scan", gi - SRING)))], inc=sc)
                        E[("dBu_g", bn, c)] = kb.emit(
                            "G", lambda g, c=c, npp=npp, gp=gp: g.tensor_mul(
                                out=dBu[gp][:], in0=dublk[:, c, :], in1=Bbc[npp][:]),
                            waits=[E[("dublk", b)], E[("bcB", bn)],
                                   (vc, E.get(("scan", gi - SRING)))], inc=gc)
                        E[("scan", gi)] = kb.emit(
                            "V", lambda v, c=c, n=n, gp=gp: v.tensor_tensor_scan(
                                out=hser[:, c, :], data0=dA[gp][:], data1=dBu[gp][:],
                                initial=h_state[:, c, n:n + 1], op0=ALU.mult, op1=ALU.add),
                            waits=[(sc, E[("dA", gi)]), (gc, E[("dBu_g", bn, c)]),
                                   (gc, E.get(("hst", bn - NS))),
                                   (vc, E.get(("yadd", bn - 1))), (vc, E["hzero"])],
                            inc=vc)

                emit_gymul(b, b * NS + NS - 1)
                emit_yadds(b, b * NS + NS - 1)
                E[("dAlast", b)] = E[("dA", (b * NS + NS - 1) * 2 + 1)]
                E[("dBulast", b)] = E[("dBu_g", b * NS + NS - 1, 1)]
                E[("wag", b)] = dma(
                    "S", lambda s, b=b: s.dma_start(
                        out=ag_in.ap()[b].rearrange("cl p t -> p cl t"), in_=yacc[:]),
                    [(vc, E[("yadd", b * NS + NS - 1)])])
                E[("uacc", b)] = dma(
                    "G", lambda g, b=b, t0=t0: g.dma_start(
                        out=ag_in.ap()[b], in_=d_uD[:, :, t0:t0 + TB], accum_op=ALU.add),
                    [E[("wag", b)], E[("wuD", last_it)]])
                E[("coll", b)] = kb.emit(
                    "G", lambda g, b=b: g.collective_compute(
                        "AllGather", ALU.bypass, replica_groups=[list(range(NCORE))],
                        ins=[ag_in.ap()[b].opt()], outs=[ag_out.ap()[b].opt()]),
                    waits=[E[("uacc", b)]], inc=ccs)

            for b in range(NB):
                sc_block(b)
            ffp.drain()

            MAIN = [(tc, tc.v), (vc, vc.v), (sc, sc.v), (gc, gc.v)] + allq()
            kb.replay(block)

        with ExitStack() as post:
            epo = post.enter_context
            m_sb = epo(nc.sbuf_tensor("m_sb", [128, NC, TOK], F32))
            x_re = m_sb

            def mk_xre(q):
                return q.dma_start(
                    out=x_re.ap().rearrange("p c (a b) -> p c a b", a=4),
                    in_=x_own_in.rearrange("a p c b -> p c a b"))
            E["xre"] = dma("Q", mk_xre, MAIN)
            E["addx"] = kb.emit(
                "V", lambda v: v.tensor_add(
                    out=out_acc.ap().rearrange("p c t -> p (c t)"),
                    in0=out_acc.ap().rearrange("p c t -> p (c t)"),
                    in1=x_re.ap().rearrange("p c t -> p (c t)")),
                waits=[E["xre"]] + MAIN, inc=vc)

            def mk_mload(q):
                rk = _pid(q)
                return q.dma_start(
                    out=m_sb[:],
                    in_=ag_out.ap().rearrange("b c p t -> p (b c) t")[:, ds(rk * NC, NC), :])
            E["mload"] = dma("Q", mk_mload,
                             [(ccs, E[("coll", NB - 1)]), (vc, E["addx"])] + MAIN)

            if "m" in dbg:
                dma("Q", lambda q: q.dma_start(
                    out=dbg_aps["dbg_m"], in_=ag_in.ap().rearrange("b cl p t -> cl p (b t)")),
                    [(ccs, E[("coll", NB - 1)])])
                E["msbd"] = dma("Q", lambda q: q.dma_start(out=dbg_aps["dbg_msb"], in_=m_sb[:]),
                                [E["mload"]])

            E["addm"] = kb.emit(
                "V", lambda v: v.tensor_add(
                    out=out_acc.ap().rearrange("p c t -> p (c t)"),
                    in0=out_acc.ap().rearrange("p c t -> p (c t)"),
                    in1=m_sb.ap().rearrange("p c t -> p (c t)")),
                waits=[E.get("msbd"), E["mload"], (vc, E["addx"])], inc=vc)

            fv = dma("Q", lambda q: q.dma_start(out=out_ap, in_=out_acc[:]),
                     [(vc, E["addm"])])
            kb.emit("Q", lambda q: None, waits=[fv])

            kb.replay(block)

    return nc


_NC_CACHE = {}


def _get_nc(upto="all", dbg=()):
    key = (upto, tuple(sorted(dbg)))
    if key not in _NC_CACHE:
        _NC_CACHE[key] = build_nc(upto, dbg)
    return _NC_CACHE[key]


def _prep_inputs(L0_attn_x, L0_attn_W_bc, L0_attn_W_dt, L0_attn_b_dt, L0_attn_A_log,
                 L0_attn_D_param, L0_ffn_gate_w, L0_ffn_expert_bias, L0_ffn_shared_gate_w,
                 L0_ffn_shared_up_w, L0_ffn_shared_down_w, L0_ffn_experts_gate_up,
                 L0_ffn_experts_down):
    f32 = np.float32
    A = np.ascontiguousarray

    x = np.asarray(L0_attn_x, f32).reshape(T, D)
    xT = A(x.T)
    x_in = A(xT.reshape(NC, 128, NPC, PC).transpose(2, 1, 0, 3))

    wbc = A(np.asarray(L0_attn_W_bc, f32).T.reshape(NC, 128, 128).transpose(1, 0, 2))
    gw = A(np.asarray(L0_ffn_gate_w, f32).reshape(NC, 128, NE).transpose(1, 0, 2))
    ebias = A(np.broadcast_to(np.asarray(L0_ffn_expert_bias, f32)[None, :], (128, NE)))
    ident = np.eye(128, dtype=f32)

    def gu_layout(w):
        a = np.asarray(w, f32).reshape(NC, 128, 8, 256).transpose(2, 1, 0, 3)
        order = [0, 4, 1, 5, 2, 6, 3, 7]
        return A(a[order])

    def dn_layout(w):
        a = np.asarray(w, f32).reshape(8, 128, 8, 256).transpose(2, 1, 0, 3)
        a = a.reshape(4, 2, 128, 8, 256).transpose(0, 2, 3, 1, 4)
        return A(a)

    sh_gu = gu_layout(np.concatenate(
        [np.asarray(L0_ffn_shared_gate_w, f32), np.asarray(L0_ffn_shared_up_w, f32)], axis=1))
    sh_d = dn_layout(np.asarray(L0_ffn_shared_down_w, f32))
    ex_gu = A(np.stack([gu_layout(np.asarray(L0_ffn_experts_gate_up, f32)[k]) for k in range(NE)]))
    ex_d = A(np.stack([dn_layout(np.asarray(L0_ffn_experts_down, f32)[k]) for k in range(NE)]))

    common = dict(x_in=x_in, wbc_in=wbc, gw_in=gw, ebias_in=ebias, ident_in=ident,
                  wsh_gu_in=sh_gu, wsh_d_in=sh_d, wexp_gu_in=ex_gu, wexp_d_in=ex_d)

    x4 = x_in.reshape(NCORE, 4, 128, NC, PC)
    Wdt = np.asarray(L0_attn_W_dt, f32)
    bdt = np.asarray(L0_attn_b_dt, f32)
    Alog = np.asarray(L0_attn_A_log, f32)
    Dpar = np.asarray(L0_attn_D_param, f32)
    in_maps = []
    for k in range(NCORE):
        sl = slice(256 * k, 256 * (k + 1))
        im = dict(common)
        im["wdt_in"] = A(Wdt[sl].T.reshape(RK, 2, 128))
        im["bdt_in"] = A(bdt[sl].reshape(2, 128).T)
        im["alog_in"] = A(Alog[sl].reshape(2, 128, NS).transpose(1, 0, 2))
        im["dpar_in"] = A(Dpar[sl].reshape(2, 128).T)
        im["x_own_in"] = A(x4[k])
        im["x_sl_in"] = A(xT[sl].reshape(2, 128, NPC, PC).transpose(2, 1, 0, 3))
        in_maps.append(im)
    return in_maps


def kernel(L0_attn_x, L0_attn_z, L0_attn_W_bc, L0_attn_W_dt, L0_attn_b_dt, L0_attn_A_log,
           L0_attn_D_param, L0_ffn_gate_w, L0_ffn_expert_bias, L0_ffn_shared_gate_w,
           L0_ffn_shared_up_w, L0_ffn_shared_down_w, L0_ffn_experts_gate_up,
           L0_ffn_experts_down):
    in_maps = _prep_inputs(
        L0_attn_x, L0_attn_W_bc, L0_attn_W_dt, L0_attn_b_dt, L0_attn_A_log,
        L0_attn_D_param, L0_ffn_gate_w, L0_ffn_expert_bias, L0_ffn_shared_gate_w,
        L0_ffn_shared_up_w, L0_ffn_shared_down_w, L0_ffn_experts_gate_up,
        L0_ffn_experts_down)
    nc = _get_nc()
    res = run_bass_kernel_spmd(nc, in_maps, core_ids=list(range(NCORE)))
    out = np.empty((1, T, D), np.float32)
    for k in range(NCORE):
        o = res.results[k]["out"]
        out[0, TOK * k:TOK * (k + 1), :] = o.transpose(2, 1, 0).reshape(TOK, D)
    return out
